# version 08
# baseline (speedup 1.0000x reference)
import os
import numpy as np
import ml_dtypes
import concourse.bass as bass
import concourse.mybir as mybir
from concourse.bass_utils import run_bass_kernel_spmd

F32 = mybir.dt.float32
BF16 = mybir.dt.bfloat16
I32 = mybir.dt.int32
AF = mybir.ActivationFunctionType
ALU = mybir.AluOpType
BF = ml_dtypes.bfloat16

S, H, NHP, FI = 4096, 1024, 8, 2816
NCH, OWN, NQT, QT = 32, 2048, 4, 512
NG = NCH // 2
DEPTH = (8, 16, 24, 32)
TOT_SUPERS = sum(d // 2 for d in DEPTH)
CEXP = 3.0
LNEPS = 1e-5
MID = (0.25, 0.75, 1.25, 1.75, 2.5, 3.5, 5.0)
NTT = 16
NS = 6
DBG = 0

VQ = ([("wg", i * 128, 0, H) for i in range(22)] +
      [("wu", i * 128, 0, H) for i in range(22)] +
      [("wd", (i // 2) * 128, (i % 2) * 1408, 1408) for i in range(16)])
NVQ = len(VQ)
VQ_W = 1408
ENGS = ("sync", "vector", "scalar", "tensor", "gpsimd")
DMA_POOL = {"DS": 12, "DG": 6, "DA": 4, "DW": 8}
SEMS = (("T", "V", "SC", "G")
        + tuple(f"{q}{i}" for q, n in DMA_POOL.items() for i in range(n)))
FOLD_Q = [[0, 3, 4, 7], [1, 2, 5, 6]]
VQ_HEAD = 4
VQ_PER_QT = (1, 1, 2, 3)


class P:
    def __init__(self):
        self.cnt = {n: 0 for n in SEMS}
        self.M = {}
        self.stream = {e: [] for e in ENGS}
        self.split = {}
        self.qrot = {}
        self.curgrp = {}
        self.qreg = None

    def e(self, eng, fn, inc=None, key=None, waits=()):
        sname = None
        n = 0
        if inc is not None:
            sname, n = (inc, 1) if isinstance(inc, str) else inc
            self.cnt[sname] += n
        if key is not None:
            assert key not in self.M, ("dup milestone", key)
            self.M[key] = (sname, self.cnt[sname])
        self.stream[eng].append((tuple(waits), fn, sname, n))

    def d(self, eng, fn, q="DS", key=None, waits=(), samegroup=False):
        if not samegroup:
            i = self.qrot[q] = (self.qrot.get(q, -1) + 1) % DMA_POOL[q]
            s = f"{q}{i}"
            self.curgrp[q] = s
            prev = self.cnt[s]
            waits = tuple(waits) + ((("_raw", s, prev),) if prev > 0 else ())
        else:
            s = self.curgrp[q]
        self.e(eng, fn, inc=(s, 16), key=key, waits=waits)

    def alias(self, newkey, oldkey):
        assert newkey not in self.M
        self.M[newkey] = self.M[oldkey]

    def mark_split(self):
        for e in ENGS:
            self.split[e] = len(self.stream[e])

    def barrier_waits(self):
        return [("_raw", s, self.cnt[s]) for s in SEMS]

    def replay(self, engname, eng, sems, phase):
        lo = 0 if phase == 1 else self.split[engname]
        hi = self.split[engname] if phase == 1 else len(self.stream[engname])
        for waits, fn, sname, n in self.stream[engname][lo:hi]:
            for w in waits:
                if isinstance(w, tuple) and len(w) == 3 and w[0] == "_raw":
                    ws, val = w[1], w[2]
                else:
                    ws, val = self.M[w]
                if val > 0:
                    eng.wait_ge(sems[ws], val)
            inst = fn(eng)
            if sname is not None:
                assert inst is not None, (engname, sname)
                inst.then_inc(sems[sname], n)


def quant_stairs(eng, xin, out, width, rinv, scl, t_t, sgn_t, u1):
    nb = width // 16
    rb = rinv.unsqueeze(2).broadcast_to([128, nb, 16])
    sb_ = scl.unsqueeze(2).broadcast_to([128, nb, 16])
    tt = t_t[:, 0:width]
    sg = sgn_t[:, 0:width]
    uu1 = u1
    eng.tensor_tensor(out=tt.rearrange("p (b k) -> p b k", k=16),
                      in0=xin.rearrange("p (b k) -> p b k", k=16),
                      in1=rb, op=ALU.mult)
    eng.tensor_scalar(out=sg, in0=tt, scalar1=0.0, scalar2=0.5,
                      op0=ALU.is_ge, op1=ALU.subtract)
    eng.scalar_tensor_tensor(out=tt, in0=tt, scalar=2.0, in1=sg,
                             op0=ALU.mult, op1=ALU.mult)
    eng.tensor_scalar(out=uu1, in0=tt, scalar1=MID[0], scalar2=None,
                      op0=ALU.is_gt)
    for m in (MID[1], MID[2], MID[3]):
        eng.scalar_tensor_tensor(out=uu1, in0=tt, scalar=float(m), in1=uu1,
                                 op0=ALU.is_gt, op1=ALU.add)
    eng.tensor_scalar(out=out, in0=tt, scalar1=MID[4], scalar2=None,
                      op0=ALU.is_gt)
    eng.scalar_tensor_tensor(out=out, in0=tt, scalar=MID[5], in1=out,
                             op0=ALU.is_gt, op1=ALU.add)
    eng.scalar_tensor_tensor(out=out, in0=tt, scalar=MID[6], in1=out,
                             op0=ALU.is_gt, op1=ALU.add)
    eng.scalar_tensor_tensor(out=uu1, in0=uu1, scalar=0.5, in1=out,
                             op0=ALU.mult, op1=ALU.add)
    eng.scalar_tensor_tensor(out=uu1, in0=tt, scalar=MID[6], in1=uu1,
                             op0=ALU.is_gt, op1=ALU.add)
    eng.tensor_tensor(out=uu1, in0=uu1, in1=sg, op=ALU.mult)
    return eng.scalar_tensor_tensor(
        out=out.rearrange("p (b k) -> p b k", k=16),
        in0=uu1.rearrange("p (b k) -> p b k", k=16),
        scalar=2.0, in1=sb_, op0=ALU.mult, op1=ALU.mult)


MID2 = tuple(float(m) * float(m) for m in MID)


def quant_prep(eng, xin, width, bm, bm2, junk):
    nb = width // 16
    eng.tensor_reduce(out=bm[:, 0:nb],
                      in_=xin.rearrange("p (b k) -> p b k", k=16),
                      axis=mybir.AxisListType.X, op=ALU.max,
                      apply_absolute_value=True)
    eng.tensor_scalar(out=junk[:, 0:nb], in0=xin[:, 0:nb], scalar1=1.0,
                      scalar2=None, op0=ALU.mult)
    eng.tensor_scalar(out=bm[:, 0:nb], in0=bm[:, 0:nb], scalar1=1.0 / 6.0,
                      scalar2=1e-12, op0=ALU.mult, op1=ALU.max)
    eng.tensor_scalar(out=junk[:, 0:nb], in0=xin[:, 0:nb], scalar1=1.0,
                      scalar2=None, op0=ALU.mult)
    eng.reciprocal(out=bm2[:, 0:nb], in_=bm[:, 0:nb])
    return eng.tensor_scalar(out=junk[:, 0:nb], in0=xin[:, 0:nb], scalar1=1.0,
                             scalar2=None, op0=ALU.mult)



def record(p, tn):
    rec_sync_p0(p, tn)
    rec_gp_p12(p, tn)
    rec_tensor_p12(p, tn)
    rec_vector_p12(p, tn)
    rec_scalar_p12(p, tn)
    rec_sync_p12(p, tn)
    p.mark_split()
    rec_p34(p, tn)


def rec_sync_p0(p, tn):
    p.d("sync", lambda s: s.dma_start(out=tn.cfg_sb[:], in_=tn.cfgq), key="cfg")
    p.d("sync", lambda s: s.dma_start(out=tn.ident[:], in_=tn.ident_in))
    p.d("sync", lambda s: s.dma_start(out=tn.sin_sb[:], in_=tn.sintab),
        samegroup=True)
    p.d("sync", lambda s: s.dma_start(out=tn.cos_sb[:], in_=tn.costab),
        samegroup=True, key="tabs")

    def ld(s):
        p.qreg = [s.value_load(tn.cfg_sb[0:1, j:j + 1]) for j in range(NQT)]
        return None
    p.e("sync", ld, waits=["cfg"])


def rec_gp_p12(p, tn):
    def vq_load(i):
        if i >= NVQ:
            return
        wname, r0, c0, width = VQ[i]
        src_ = {"wg": tn.wg_in, "wu": tn.wu_in, "wd": tn.wd_in}[wname]
        b = i % 4
        p.d("gpsimd", lambda g, src_=src_, r0=r0, c0=c0, width=width, b=b:
            g.dma_start(out=tn.vq_in[:, b * VQ_W:b * VQ_W + width],
                        in_=src_[r0:r0 + 128, c0:c0 + width]), q="DW",
            key=("vqin", i))

    vq_done = [0]

    def vq_io(upto):
        while vq_done[0] < min(upto, NVQ):
            i = vq_done[0]
            wname, r0, c0, width = VQ[i]
            dst = {"wg": tn.wgq_scr, "wu": tn.wuq_scr, "wd": tn.wdq_scr}[wname]
            sw = [("vqrs", i + 1)] if i + 1 < NVQ else ["vqflush"]
            p.d("gpsimd", lambda g, dst=dst, r0=r0, c0=c0, width=width, i=i:
                g.dma_start(out=dst[r0:r0 + 128, c0:c0 + width],
                            in_=tn.vq_out[:, (i % 2) * VQ_W:
                                          (i % 2) * VQ_W + width]), q="DW",
                key=("vqsp", i), waits=sw)
            vq_load(i + 4)
            vq_done[0] += 1

    vq_load(0)
    vq_load(1)
    for i in range(4):
        p.d("gpsimd", lambda g, i=i: g.dma_start(
            out=tn.xbf_scr[i * 1024:(i + 1) * 1024, :],
            in_=tn.x_all[i * 1024:(i + 1) * 1024, :]), q="DG",
            samegroup=(i > 0), key="xcast" if i == 3 else None)
    vq_load(2)
    vq_load(3)

    p.d("gpsimd", lambda g: g.dma_start(out=tn.wqkv_bf, in_=tn.wqkv),
        q="DG", key="wqkvcast")
    cum = VQ_HEAD
    for hp in range(NHP):
        for j4 in range(NQT):
            if hp >= 1:
                maskp_qt(p, tn, hp - 1, j4)
            cum += VQ_PER_QT[j4]
            vq_io(cum - 2)
    for j4 in range(NQT):
        maskp_qt(p, tn, NHP - 1, j4)
        vq_io(NVQ - 2 if j4 < NQT - 1 else NVQ)


def rec_tensor_p12(p, tn):
    for hp in range(NHP):
        att = attention_qt_items(p, tn, hp - 1) if hp >= 1 else [[], [], [], []]
        for j4 in range(NQT):
            for c in range(8 * j4, 8 * j4 + 8):
                qkv_chunk_item(p, tn, hp, c)()
            for fn in att[j4]:
                fn()
    for blk in attention_qt_items(p, tn, NHP - 1):
        for fn in blk:
            fn()


def qkv_chunk_item(p, tn, hp, c):
    buf = hp % 2
    waits = []
    if c == 0:
        waits += ["xT", ("wqkv", hp)]
        if hp >= 1:
            waits.append(("qout", hp - 1))
            waits.append(("qkout", hp - 1))
        if hp >= 2:
            waits.append(("pvs", hp - 2, TOT_SUPERS - 1))
    elif c == 1:
        if hp >= 1:
            waits.append(("qevict", hp - 1, NCH - 1))
    else:
        waits.append(("qevict", hp, c - 2))

    def qmm(te, hp=hp, c=c, buf=buf):
        last = None
        h0 = (c % 2) * 512
        for hc in range(8):
            last = te.matmul(
                tn.qkv_ps[:, h0:h0 + 384],
                tn.xT[:, hc * S + c * 128: hc * S + (c + 1) * 128],
                tn.wqkv_sb[:, buf * 3072 + hc * 384:
                           buf * 3072 + (hc + 1) * 384],
                start=(hc == 0), stop=(hc == 7))
        return last
    return lambda: p.e("tensor", qmm, inc="T", key=("qkvmm", hp, c),
                       waits=waits)


def attention_qt_items(p, tn, hp):
    buf = hp % 2
    blocks = []
    sup = 0
    for j in range(NQT):
        items = []
        D = DEPTH[j]
        for s in range(D // 2):
            waits = []
            if sup == 0:
                waits.append(("qT", hp))
                waits.append(("qkT", hp))
            if sup > 0:
                waits.append(("exp", hp, sup - 1, 1))

            def qk(te, hp=hp, j=j, s=s, buf=buf):
                lastB = None
                for ii in range(2):
                    i = 2 * s + ii
                    for head in range(2):
                        dst = tn.sco_a if head == 0 else tn.sco_b
                        lm = te.matmul(
                            dst[:, ii * 512:(ii + 1) * 512],
                            tn.kT_sb[head * 64:(head + 1) * 64,
                                     buf * S + i * 128: buf * S + (i + 1) * 128],
                            tn.qT_sb[head * 64:(head + 1) * 64,
                                     buf * OWN + j * QT: buf * OWN + (j + 1) * QT],
                            start=True, stop=True, tile_position=(head * 64, 0))
                        if head == 1:
                            lastB = lm
                return lastB
            items.append(lambda qk=qk, hp=hp, sup=sup, waits=tuple(waits):
                         p.e("tensor", qk, inc="T", key=("qk", hp, sup),
                             waits=waits))
            if s > 0:
                items.append(pv_super_item(p, tn, hp, j, s - 1, sup - 1))
            sup += 1
        items.append(pv_super_item(p, tn, hp, j, D // 2 - 1, sup - 1))
        items.append(lambda p=p, hp=hp, j=j, sup=sup:
                     p.alias(("pvq", hp, j), ("pvs", hp, sup - 1)))
        blocks.append(items)
    return blocks


def pv_super_item(p, tn, hp, j, s, sup):
    D = DEPTH[j]
    buf = hp % 2
    waits = [("exp", hp, sup, 0), ("exp", hp, sup, 1), ("maskv", hp, sup)]
    if s == 0:
        if j > 0:
            waits.append(("normq", hp, j - 1))
        elif hp > 0:
            waits.append(("normq", hp - 1, NQT - 1))
    pbuf = sup % 2

    def pv(te, hp=hp, j=j, s=s, pbuf=pbuf, buf=buf, D=D):
        last = None
        for ii in range(2):
            i = 2 * s + ii
            for head in range(2):
                pvp = tn.pv_a if head == 0 else tn.pv_b
                pt = tn.pT_sb[:, (head * 2 + pbuf) * 1024 + ii * 512:
                              (head * 2 + pbuf) * 1024 + (ii + 1) * 512]
                te.matmul(pvp[0:64, :],
                          tn.vaug[:, buf * NCH * 130 + i * 130 + head * 65:
                                  buf * NCH * 130 + i * 130 + head * 65 + 64],
                          pt, start=(i == 0), stop=(i == D - 1),
                          tile_position=(0, 0), skip_group_check=True)
                last = te.matmul(pvp[64:128, :], tn.ones64[:], pt,
                                 start=(i == 0), stop=(i == D - 1),
                                 tile_position=(0, 64), skip_group_check=True)
        return last
    return lambda: p.e("tensor", pv, inc="T", key=("pvs", hp, sup),
                       waits=waits)


def rec_vector_p12(p, tn):
    vq_next = [0]

    def do_vq(nmax):
        done = 0
        while vq_next[0] < NVQ and done < nmax:
            i = vq_next[0]
            wname, r0, c0, width = VQ[i]
            b = i % 4

            def prep(v, i=i, width=width, b=b):
                return quant_prep(v, tn.vq_in[:, b * VQ_W:b * VQ_W + width],
                                  width, tn.vq_bm, tn.vq_bm2, tn.vq_t)
            p.e("vector", prep, inc="V", key=("vqrs", i), waits=[("vqin", i)])

            def qf(v, i=i, width=width, b=b):
                hw = width // 2
                last = None
                for off in (0, hw):
                    xin = tn.vq_in[:, b * VQ_W + off:b * VQ_W + off + hw]
                    last = quant_stairs(
                        v, xin,
                        tn.vq_out[:, (i % 2) * VQ_W + off:
                                  (i % 2) * VQ_W + off + hw],
                        hw, tn.vq_bm2[:, off // 16:(off + hw) // 16],
                        tn.vq_bm[:, off // 16:(off + hw) // 16],
                        tn.vq_t, tn.vq_u2, tn.vq_u1[:, 0:hw])
                return last
            qw = [("vqsp", i - 2)] if i >= 2 else []
            p.e("vector", qf, inc="V", key=("vqq", i), waits=qw)
            vq_next[0] += 1
            done += 1
    p.e("vector", lambda v: v.memset(tn.ones64[:], 1.0), inc="V", key="ones")
    do_vq(VQ_HEAD)
    for hp in range(NHP):
        buf = hp % 2

        def ev(v, hp=hp, c=None, buf=buf, first=None):
            if first:
                v.memset(tn.vaug[:, buf * NCH * 130:(buf + 1) * NCH * 130], 1.0)
            h0 = (c % 2) * 512
            qk = tn.qk_tok[:]
            v.tensor_copy(
                out=qk.rearrange("p (c r) -> p c r", r=256)[:, c:c + 1, :],
                in_=tn.qkv_ps[:].rearrange("p (c r) -> p c r", r=512)[
                    :, c % 2:c % 2 + 1, 0:256])
            return v.tensor_copy(
                out=tn.vaug[:, buf * NCH * 130:(buf + 1) * NCH * 130].rearrange(
                    "p (c h sx) -> p c h sx", h=2, sx=65)[
                    :, c:c + 1, :, 0:64],
                in_=tn.qkv_ps[:].rearrange("p (c r) -> p c r", r=512)[
                    :, c % 2:c % 2 + 1, 256:384].rearrange(
                        "p c (h d) -> p c h d", d=64))

        def rope(v, half=0):
            qk = tn.qk_tok[:].rearrange("p (c qk r) -> p c qk r", qk=2, r=128)
            c0, c1 = half * 16, (half + 1) * 16
            Ev = qk[:, c0:c1, :, 0:32]
            Ov = qk[:, c0:c1, :, 32:64]
            cosv = tn.cos_sb[:].rearrange("p (c w) -> p c w", w=32)[
                :, c0:c1].unsqueeze(2).broadcast_to([128, 16, 2, 32])
            sinv = tn.sin_sb[:].rearrange("p (c w) -> p c w", w=32)[
                :, c0:c1].unsqueeze(2).broadcast_to([128, 16, 2, 32])
            t1 = tn.rope_t1[:].rearrange("p (c qk r) -> p c qk r", qk=2, r=32)
            t2 = tn.rope_t2[:].rearrange("p (c qk r) -> p c qk r", qk=2, r=32)
            t3 = tn.rope_t3[:].rearrange("p (c qk r) -> p c qk r", qk=2, r=32)
            v.tensor_tensor(out=t1, in0=Ev, in1=cosv, op=ALU.mult)
            v.tensor_tensor(out=t2, in0=Ov, in1=sinv, op=ALU.mult)
            v.tensor_tensor(out=t1, in0=t1, in1=t2, op=ALU.subtract)
            v.tensor_tensor(out=t2, in0=Ev, in1=sinv, op=ALU.mult)
            v.tensor_tensor(out=t3, in0=Ov, in1=cosv, op=ALU.mult)
            v.tensor_tensor(out=Ov, in0=t2, in1=t3, op=ALU.add)
            return v.tensor_copy(out=Ev, in_=t1)

        def rope_end(v):
            rope(v, 1)
            return v.tensor_scalar(out=tn.rope_t1[:, 0:16],
                                   in0=tn.rope_t2[:, 0:16], scalar1=1.0,
                                   scalar2=None, op0=ALU.mult)

        for j4 in range(NQT):
            for c in range(8 * j4, 8 * j4 + 8):
                waits = [("qkvmm", hp, c)]
                if c == 0 and hp >= 2:
                    waits.append(("pvs", hp - 2, TOT_SUPERS - 1))
                p.e("vector",
                    (lambda v, hp=hp, c=c, buf=buf, first=(c == 0):
                     ev(v, hp, c, buf, first)),
                    inc="V", key=("qevict", hp, c), waits=waits)
            if j4 == 3:
                p.e("vector", lambda v: rope(v, 0),
                    waits=["tabs", ("qevict", hp, NCH - 1)])
                p.e("vector", rope_end, inc="V", key=("rope", hp))
            if hp >= 1:
                attv_qt(p, tn, hp - 1, j4)
            do_vq(VQ_PER_QT[j4])
        if hp >= 1:
            p.alias(("norm", hp - 1), ("normq", hp - 1, NQT - 1))
    for j4 in range(NQT):
        attv_qt(p, tn, NHP - 1, j4)
    p.alias(("norm", NHP - 1), ("normq", NHP - 1, NQT - 1))
    do_vq(NVQ)
    p.e("vector", lambda v: v.tensor_scalar(
        out=tn.rope_t1[:, 0:16], in0=tn.rope_t2[:, 0:16], scalar1=1.0,
        scalar2=None, op0=ALU.mult), inc="V", key="vqflush")


def maskp_qt(p, tn, hp, j):
    sup = sum(DEPTH[jj] // 2 for jj in range(j))
    D = DEPTH[j]
    for s in range(D // 2):
        pbuf = sup % 2
        masked = [2 * s + ii for ii in range(2) if 2 * s + ii >= D - 8]
        if masked:
            def mk(g, s=s, pbuf=pbuf, masked=tuple(masked), D=D):
                for head in range(2):
                    for i in masked:
                        m = i - (D - 8)
                        ii = i - 2 * s
                        pt = tn.pT_sb[:, (head * 2 + pbuf) * 1024 + ii * 512:
                                      (head * 2 + pbuf) * 1024 + (ii + 1) * 512]
                        g.tensor_tensor(
                            out=pt, in0=pt,
                            in1=tn.mask_sb[:, m * QT:(m + 1) * QT],
                            op=ALU.mult)
                return g.tensor_scalar(out=tn.pjunk[:], in0=tn.mask_sb[:, 0:16],
                                       scalar1=1.0, scalar2=None,
                                       op0=ALU.mult)
            p.e("gpsimd", mk, inc="G", key=("maskv", hp, sup),
                waits=[("exp", hp, sup, 0), ("exp", hp, sup, 1),
                       ("mload", hp, j)])
        else:
            p.M[("maskv", hp, sup)] = ("G", p.cnt["G"])
        sup += 1


def attv_qt(p, tn, hp, j):
    buf = hp % 2

    def sums(v):
        v.tensor_copy(out=tn.rs_sb[0:64, :], in_=tn.pv_a[64:128, :])
        return v.tensor_copy(out=tn.rs_sb[64:128, :], in_=tn.pv_b[64:128, :])
    p.e("vector", sums, inc="V", key=("sums", hp, j), waits=[("pvq", hp, j)])

    nw = [("recip", hp, j)]
    if j == 0 and hp >= 1:
        nw.append(("attnsp", hp - 1))

    def norm(v, j=j):
        v.tensor_tensor(
            out=tn.attnT_sb[0:64, j * QT:(j + 1) * QT],
            in0=tn.pv_a[0:64, :], in1=tn.rr_sb[0:64, :], op=ALU.mult)
        v.tensor_tensor(
            out=tn.attnT_sb[64:128, j * QT:(j + 1) * QT],
            in0=tn.pv_b[0:64, :], in1=tn.rr_sb[64:128, :], op=ALU.mult)
        return v.tensor_scalar(out=tn.rope_t3[:, 0:16],
                               in0=tn.rope_t2[:, 0:16], scalar1=1.0,
                               scalar2=None, op0=ALU.mult)
    p.e("vector", norm, inc="V", key=("normq", hp, j), waits=nw)


def rec_scalar_p12(p, tn):
    def wqkv_ld(hp):
        if hp >= NHP:
            return
        buf = hp % 2
        w = ["wqkvcast"] if hp < 2 else [("qkvmm", hp - 2, NCH - 1)]
        p.d("scalar", lambda s, hp=hp, buf=buf: s.dma_start(
            out=tn.wqkv_sb[:, buf * 3072:(buf + 1) * 3072].rearrange(
                "p (hc c) -> p hc c", hc=8),
            in_=tn.wqkv_bf.rearrange("(hc pp) c -> pp hc c", pp=128)[
                :, :, hp * 384:(hp + 1) * 384]), q="DA",
            key=("wqkv", hp), waits=w)

    wqkv_ld(0)
    wqkv_ld(1)
    for hp in range(NHP):
        wqkv_ld(hp + 2)
        sup = 0
        for j in range(NQT):
            D = DEPTH[j]
            mwaits = []
            if hp * NQT + j >= 1:
                jj = hp * NQT + j - 1
                mwaits.append(("normq", jj // NQT, jj % NQT))
            p.d("scalar", lambda s, j=j: s.dma_start(
                out=tn.mask_sb[:].rearrange("p (m q) -> p m q", m=8),
                in_=tn.masks_in.rearrange("(j m pp) q -> pp j m q",
                                          m=8, pp=128)[:, j]),
                q="DA", key=("mload", hp, j), waits=mwaits)
            for s in range(D // 2):
                pbuf = sup % 2
                for head in range(2):
                    waits = [("qk", hp, sup)]
                    if sup >= 2:
                        waits.append(("pvs", hp, sup - 2))

                    def ex(sc, head=head, pbuf=pbuf):
                        src = tn.sco_a if head == 0 else tn.sco_b
                        return sc.activation(
                            out=tn.pT_sb[:, (head * 2 + pbuf) * 1024:
                                         (head * 2 + pbuf + 1) * 1024],
                            in_=src[:], func=AF.Exp,
                            bias=np.float32(-CEXP), scale=np.float32(0.125))
                    p.e("scalar", ex, inc="SC", key=("exp", hp, sup, head),
                        waits=waits)
                sup += 1

            def recip(sc):
                sc.activation(out=tn.rr_sb[:], in_=tn.rs_sb[:], func=AF.Ln)
                return sc.activation(out=tn.rr_sb[:], in_=tn.rr_sb[:],
                                     func=AF.Exp, scale=np.float32(-1.0))
            p.e("scalar", recip, inc="SC", key=("recip", hp, j),
                waits=[("sums", hp, j)])


def rec_sync_p12(p, tn):
    for hc in range(8):
        p.d("sync", lambda s, hc=hc: s.dma_start_transpose(
            tn.xT[:, hc * S:(hc + 1) * S],
            tn.xbf_scr[:, hc * 128:(hc + 1) * 128]), q="DS",
            waits=["xcast", "tabs"] if hc == 0 else [], samegroup=(hc > 0),
            key="xT" if hc == 7 else None)
    for hp in range(NHP):
        buf = hp % 2
        p.d("sync", lambda s, buf=buf: s.dma_start(
            out=tn.q_scr[buf].rearrange("(c pp) d -> pp c d", pp=128),
            in_=tn.qk_tok[:].rearrange("p (c r) -> p c r", r=256)[:, :, 0:128]),
            q="DS", waits=[("rope", hp)], key=("qout", hp))
        p.d("sync", lambda s, buf=buf: s.dma_start(
            out=tn.k_scr[buf].rearrange("(c pp) d -> pp c d", pp=128),
            in_=tn.qk_tok[:].rearrange("p (c r) -> p c r", r=256)[:, :, 128:256]),
            q="DS", key=("qkout", hp))
        first = True
        for j in range(NQT):
            for hh in range(2):
                p.d("sync", lambda s, j=j, buf=buf, hh=hh: s.dma_start(
                    out=tn.qown_scr[buf][j * QT:(j + 1) * QT,
                                         hh * 64:(hh + 1) * 64],
                    in_=tn.q_scr[buf][bass.ts(p.qreg[j], QT), :].rearrange(
                        "r (g h k) -> r h g k", g=4, h=2)[:, hh]), q="DS",
                    waits=[("qout", hp)] if first else [],
                    samegroup=not first,
                    key=("qgather", hp) if (j == NQT - 1 and hh == 1) else None)
                first = False
        for hh in range(2):
            p.d("sync", lambda s, buf=buf, hh=hh: s.dma_start(
                out=tn.kperm_scr[buf][:, hh * 64:(hh + 1) * 64],
                in_=tn.k_scr[buf].rearrange(
                    "r (g h k) -> r h g k", g=4, h=2)[:, hh]), q="DS",
                waits=[("qkout", hp)] if hh == 0 else [], samegroup=(hh == 1),
                key=("qkperm", hp) if hh == 1 else None)
        twaits = [("qgather", hp)]
        if hp >= 2:
            twaits.append(("pvs", hp - 2, TOT_SUPERS - 1))
        p.d("sync", lambda s, buf=buf: s.dma_start_transpose(
            tn.qT_sb[:, buf * OWN:(buf + 1) * OWN], tn.qown_scr[buf]),
            q="DS", waits=twaits, key=("qT", hp))
        p.d("sync", lambda s, buf=buf: s.dma_start_transpose(
            tn.kT_sb[:, buf * S:(buf + 1) * S], tn.kperm_scr[buf]),
            q="DS", waits=[("qkperm", hp)], key=("qkT", hp))
        p.d("sync", lambda s, hp=hp: s.dma_start(
            out=tn.attnT_scr[hp * 128:(hp + 1) * 128, :],
            in_=tn.attnT_sb[:]), q="DS", key=("attnsp", hp),
            waits=[("norm", hp)])
    if DBG:
        p.d("sync", lambda s: s.dma_start(out=tn.d_attnT, in_=tn.attnT_scr),
            q="DS", waits=[("attnsp", NHP - 1)])
        p.d("sync", lambda s: s.dma_start(out=tn.d_qk, in_=tn.qk_tok[:]), q="DS",
            samegroup=True)
        p.d("sync", lambda s: s.dma_start(
            out=tn.d_vaug, in_=tn.vaug[:, NCH * 130:2 * NCH * 130]), q="DS",
            samegroup=True, key="dbgsp")


def rec_p34(p, tn):
    bw = p.barrier_waits()
    for e in ENGS:
        p.e(e, lambda _: None, waits=bw)

    p.d("gpsimd", lambda g: g.dma_start(
        out=tn.wout_sb[:].rearrange("p (h c) -> p h c", h=8),
        in_=tn.wout.rearrange("(h pp) c -> pp h c", pp=128)), q="DG",
        key="woutld")

    allvq = [("vqsp", i) for i in range(NVQ)]
    for hc in range(8):
        p.d("sync", lambda s, hc=hc: s.dma_start_transpose(
            tn.wgT[:, hc * FI:(hc + 1) * FI],
            tn.wgq_scr[:, hc * 128:(hc + 1) * 128]), q="DS",
            waits=allvq if hc == 0 else [], samegroup=(hc > 0),
            key="wgT" if hc == 7 else None)
    for hc in range(8):
        p.d("sync", lambda s, hc=hc: s.dma_start_transpose(
            tn.wuT[:, hc * FI:(hc + 1) * FI],
            tn.wuq_scr[:, hc * 128:(hc + 1) * 128]), q="DS",
            waits=allvq if hc == 0 else [], samegroup=(hc > 0),
            key="wguT" if hc == 7 else None)
    for ncc in range(22):
        p.d("sync", lambda s, ncc=ncc: s.dma_start_transpose(
            tn.wdT[:, ncc * H:(ncc + 1) * H],
            tn.wdq_scr[:, ncc * 128:(ncc + 1) * 128]), q="DS",
            samegroup=(ncc % 8 != 0),
            key="wdT" if ncc == 21 else None)

    for t in range(NTT):
        tb = t % 2
        ldw = []
        if t == 0:
            ldw += [("attnsp", h) for h in range(NHP)]
        if t >= 1:
            ldw.append(("ostt", t - 1))
        p.d("sync", lambda s, t=t: s.dma_start(
            out=tn.aT_t[:].rearrange("p (h q) -> p h q", h=8),
            in_=tn.attnT_scr.rearrange("(h pp) q -> pp h q", pp=128)[
                :, :, t * 128:(t + 1) * 128]), q="DS", waits=ldw)
        p.d("sync", lambda s, t=t: s.dma_start(
            out=tn.xres_t[:], in_=tn.x_own[t * 128:(t + 1) * 128, :]), q="DS",
            key=("p3in", t), samegroup=True)

        opw = [("p3in", t), "woutld"]
        if t >= 1:
            opw.append(("ostt", t - 1))

        def opmm(te, t=t):
            last = None
            for half in range(2):
                for hp in range(NHP):
                    last = te.matmul(
                        tn.op_ps[:, half * 512:(half + 1) * 512],
                        tn.aT_t[:, hp * 128:(hp + 1) * 128],
                        tn.wout_sb[:, hp * 1024 + half * 512:
                                   hp * 1024 + half * 512 + 512],
                        start=(hp == 0), stop=(hp == NHP - 1))
            return last
        p.e("tensor", opmm, inc="T", key=("opmm", t), waits=opw)

        if DBG and t == 0:
            p.d("sync", lambda s: s.dma_start(out=tn.d_aT0, in_=tn.aT_t[:]),
                q="DS", waits=[("opmm", 0)])
            p.d("sync", lambda s: s.dma_start(out=tn.d_xres0, in_=tn.xres_t[:]),
                q="DS", samegroup=True, key="dbg_t0")
            p.d("sync", lambda s: s.dma_start(out=tn.d_ops0, in_=tn.y1_t[:]),
                q="DS", waits=[("lnastt", 0)], key="dbg_t0b")

        if DBG and t == 1:
            p.d("sync", lambda s: s.dma_start(
                out=tn.d_ln0[:, 0:4], in_=tn.lnsum[:]), q="DS",
                waits=[("lnadone", 0)])
            p.d("sync", lambda s: s.dma_start(
                out=tn.d_ln0[:, 4:8], in_=tn.lnsq[:]), q="DS", samegroup=True)
            p.d("sync", lambda s: s.dma_start(
                out=tn.d_ln0[:, 8:10], in_=tn.lnr[:]), q="DS", samegroup=True)
            p.d("sync", lambda s: s.dma_start(
                out=tn.d_y1t0, in_=tn.y1_t[:]), q="DS", samegroup=True,
                key="dbg_ln0")

        yvf = (lambda: tn.y1_t[:])
        ow = [("opmm", t)]
        if DBG and t == 0:
            ow.append("dbg_t0")
        if DBG and t == 1:
            ow.append("dbg_ln0")
        if t >= 2:
            ow.append(("p3sp", t - 2))
        if t >= 1:
            ow.append(("p3sp", t - 1))
        p.e("vector", lambda v, yvf=yvf: v.scalar_tensor_tensor(
            out=yvf(), in0=tn.op_ps[:], scalar=1.0, in1=tn.xres_t[:],
            op0=ALU.mult, op1=ALU.add, accum_out=tn.lnsum[:, 0:1]),
            inc="V", key=("ostt", t), waits=ow)
        p.alias(("lnastt", t), ("ostt", t))
        ln_emit(p, tn, "lna", t, yvf)

        def xquant(v, yvf=yvf):
            quant_prep(v, yvf(), H, tn.xbm, tn.xbm2, tn.qs1)
            quant_stairs(v, yvf(), tn.xq_t[:], H, tn.xbm2[:, 0:64],
                         tn.xbm[:, 0:64], tn.qs1, tn.qs3,
                         tn.qs2[:, 0:H])
            return v.tensor_scalar(out=tn.xbm2[:, 0:16], in0=tn.xbm[:, 0:16],
                                   scalar1=1.0, scalar2=None,
                                   op0=ALU.mult)
        p.e("vector", xquant, inc="V", key=("xq", t), waits=[("lnadone", t)])

        xtw = [("xq", t)]
        if t >= 1:
            xtw.append(("xqtev", t - 1))

        def xqt(te):
            last = None
            for hc in range(8):
                last = te.transpose(tn.xqT_ps[:, hc * 128:(hc + 1) * 128],
                                    tn.xq_t[:, hc * 128:(hc + 1) * 128],
                                    tn.ident[:])
            return last
        p.e("tensor", xqt, inc="T", key=("xqT", t), waits=xtw)
        evw = [("xqT", t)]
        if t >= 1:
            evw.append(("xqspill", t - 1))
        def xqtev(v):
            v.tensor_copy(out=tn.xqT_t[:], in_=tn.xqT_ps[:])
            return v.tensor_scalar(out=tn.xbm2[:, 16:32], in0=tn.xbm[:, 0:16],
                                   scalar1=1.0, scalar2=None,
                                   op0=ALU.mult)
        p.e("vector", xqtev, inc="V", key=("xqtev", t), waits=evw)
        p.d("sync", lambda s, t=t: s.dma_start(
            out=tn.xqT_scr.rearrange("(h pp) q -> pp h q", pp=128)[
                :, :, t * 128:(t + 1) * 128],
            in_=tn.xqT_t[:].rearrange("p (h q) -> p h q", h=8)), q="DS",
            key=("xqspill", t), waits=[("xqtev", t)])
        p.d("sync", lambda s, t=t: s.dma_start(
            out=tn.y1_scr[t * 128:(t + 1) * 128, :], in_=tn.y1_t[:]), q="DS",
            key=("p3sp", t), waits=[("xq", t)])

    if DBG:
        p.d("sync", lambda s: s.dma_start(out=tn.d_y1, in_=tn.y1_scr), q="DS",
            waits=[("p3sp", t2) for t2 in range(NTT)])
        p.d("sync", lambda s: s.dma_start(out=tn.d_wgq, in_=tn.wgq_scr), q="DS",
            samegroup=True)
        p.d("sync", lambda s: s.dma_start(out=tn.d_wuq, in_=tn.wuq_scr), q="DS",
            samegroup=True)
        p.d("sync", lambda s: s.dma_start(out=tn.d_wdq, in_=tn.wdq_scr), q="DS",
            samegroup=True)
        p.d("sync", lambda s: s.dma_start(out=tn.d_xqT, in_=tn.xqT_scr), q="DS",
            samegroup=True, key="dbgsp2")

    def emit_gumm(t, ns):
        tb = t % 2
        w0 = ns * 512
        w1 = min(FI, w0 + 512)
        wd_ = w1 - w0
        guw = [("p4x", t)]
        if ns == 0:
            guw.append("wgT")
            guw.append("wguT")
        prev = ("guev", t, ns - 2) if ns >= 2 else \
            (("guev", t - 1, NS - 2 + ns) if t >= 1 else None)
        if prev is not None:
            guw.append(prev)
        if t == 0 and ns == 1:
            guw.append(("ostt", NTT - 1))

        def gumm(te, t=t, tb=tb, ns=ns, w0=w0, w1=w1, wd_=wd_):
            gps = tn.g_ps if ns % 2 == 0 else tn.op_ps_g
            ups = tn.u_ps if ns % 2 == 0 else tn.op_ps_u
            for hc in range(8):
                te.matmul(gps[:, 0:wd_],
                          tn.xqT4_t[:, tb * 1024 + hc * 128:
                                    tb * 1024 + (hc + 1) * 128],
                          tn.wgT[:, hc * FI + w0: hc * FI + w1],
                          start=(hc == 0), stop=(hc == 7))
            last = None
            for hc in range(8):
                last = te.matmul(ups[:, 0:wd_],
                                 tn.xqT4_t[:, tb * 1024 + hc * 128:
                                           tb * 1024 + (hc + 1) * 128],
                                 tn.wuT[:, hc * FI + w0: hc * FI + w1],
                                 start=(hc == 0), stop=(hc == 7))
            return last
        p.e("tensor", gumm, inc="T", key=("gumm", t, ns), waits=guw)

    def emit_hq_half(t, half):
        o0 = half * (FI // 2)
        hw = FI // 2
        nb0 = o0 // 16

        def hquant(v, o0=o0, hw=hw, nb0=nb0):
            v.tensor_tensor(out=tn.hg_t[:, o0:o0 + hw],
                            in0=tn.hg_t[:, o0:o0 + hw],
                            in1=tn.hu_t[:, o0:o0 + hw], op=ALU.mult)
            quant_prep(v, tn.hg_t[:, o0:o0 + hw], hw,
                       tn.hbm[:, nb0:nb0 + hw // 16],
                       tn.hbm2[:, nb0:nb0 + hw // 16], tn.hq_u2)
            xin = tn.hg_t[:, o0:o0 + hw]
            quant_stairs(v, xin, tn.hq_t[:, o0:o0 + hw], hw,
                         tn.hbm2[:, nb0:nb0 + hw // 16],
                         tn.hbm[:, nb0:nb0 + hw // 16],
                         tn.hu_t, tn.hq_u2, xin)
            return v.tensor_scalar(out=tn.hbm2[:, 0:8], in0=tn.hbm[:, 0:8],
                                   scalar1=1.0, scalar2=None,
                                   op0=ALU.mult)
        ns_done = 2 if half == 0 else NS - 1
        p.e("vector", hquant, inc="V", key=("hq", t, half),
            waits=[("silu", t, ns_done)])

    def emit_p4x(t):
        if t >= NTT:
            return
        tb = t % 2
        lx = [("xqspill", t)]
        if t >= 2:
            lx.append(("gumm", t - 2, NS - 1))
        p.d("sync", lambda s, t=t, tb=tb: s.dma_start(
            out=tn.xqT4_t[:, tb * 1024:(tb + 1) * 1024].rearrange(
                "p (h q) -> p h q", h=8),
            in_=tn.xqT_scr.rearrange("(h pp) q -> pp h q", pp=128)[
                :, :, t * 128:(t + 1) * 128]), q="DS", waits=lx,
            key=("p4x", t))

    emit_p4x(0)
    emit_p4x(1)
    for t in range(NTT):
        tb = t % 2
        emit_p4x(t + 2)
        ly = [("p3sp", t)]
        if t >= 1:
            ly.append(("dstt", t - 1))
        p.d("sync", lambda s, t=t: s.dma_start(
            out=tn.y14_t[:], in_=tn.y1_scr[t * 128:(t + 1) * 128, :]), q="DS",
            key=("p4y", t), waits=ly)

        if t == 0:
            emit_gumm(0, 0)
            emit_gumm(0, 1)
        for ns in range(2, NS):
            emit_gumm(t, ns)
        if t + 1 < NTT:
            emit_gumm(t + 1, 0)
            emit_gumm(t + 1, 1)

        for ns in range(NS):
            w0 = ns * 512
            w1 = min(FI, w0 + 512)
            wd_ = w1 - w0
            sw = [("gumm", t, ns)]
            if t == 0 and ns == 0:
                sw.append(("xq", NTT - 1))
            if t >= 1 and ns == 0:
                sw.append(("hq", t - 1, 1))
            p.e("scalar", lambda s, ns=ns, w0=w0, w1=w1, wd_=wd_: s.activation(
                out=tn.hg_t[:, w0:w1],
                in_=(tn.g_ps if ns % 2 == 0 else tn.op_ps_g)[:, 0:wd_],
                func=AF.Silu), inc="SC", key=("silu", t, ns),
                waits=sw)
            p.e("vector", lambda v, ns=ns, w0=w0, w1=w1, wd_=wd_: v.tensor_copy(
                out=tn.hu_t[:, w0:w1],
                in_=(tn.u_ps if ns % 2 == 0 else tn.op_ps_u)[:, 0:wd_]),
                inc="V", key=("guev", t, ns), waits=[("gumm", t, ns)])
            if ns == 2:
                emit_hq_half(t, 0)
        emit_hq_half(t, 1)

        for grp in range(3):
            c0, c1 = grp * 8, min(22, grp * 8 + 8)
            if grp == 0:
                hw_ = [("hq", t, 0)]
                if t >= 1:
                    hw_.append(("dmm", t - 1))
            elif grp == 1:
                hw_ = [("hqtev", t, 0), ("hq", t, 1)]
            else:
                hw_ = [("hqtev", t, 1)]

            def hqt(te, c0=c0, c1=c1):
                last = None
                for ci in range(c0, c1):
                    last = te.transpose(
                        tn.hqT_ps[:, (ci - c0) * 128:(ci - c0 + 1) * 128],
                        tn.hq_t[:, ci * 128:(ci + 1) * 128], tn.ident[:])
                return last
            p.e("tensor", hqt, inc="T", key=("hqT", t, grp), waits=hw_)

            def hqtev(v, c0=c0, c1=c1):
                v.tensor_copy(out=tn.hqT_t[:, c0 * 128:c1 * 128],
                              in_=tn.hqT_ps[:, 0:(c1 - c0) * 128])
                return v.tensor_scalar(out=tn.hbm2[:, 8:16],
                                       in0=tn.hbm[:, 0:8], scalar1=1.0,
                                       scalar2=None, op0=ALU.mult)
            p.e("vector", hqtev, inc="V", key=("hqtev", t, grp),
                waits=[("hqT", t, grp)])

        dw = ["wdT", ("hqtev", t, 2)]
        if t >= 1:
            dw.append(("dstt", t - 1))

        def dmm(te):
            last = None
            for half in range(2):
                for ci in range(22):
                    last = te.matmul(
                        tn.d_ps[:, half * 512:(half + 1) * 512],
                        tn.hqT_t[:, ci * 128:(ci + 1) * 128],
                        tn.wdT[:, ci * H + half * 512: ci * H + half * 512 + 512],
                        start=(ci == 0), stop=(ci == 21))
            return last
        p.e("tensor", dmm, inc="T", key=("dmm", t), waits=dw)

        y2f = (lambda: tn.y1_t[:])
        dsw = [("dmm", t), ("p4y", t)]
        if t >= 1:
            dsw.append(("yst", t - 1))
        p.e("vector", lambda v, y2f=y2f: v.scalar_tensor_tensor(
            out=y2f(), in0=tn.d_ps[:], scalar=1.0, in1=tn.y14_t[:],
            op0=ALU.mult, op1=ALU.add, accum_out=tn.lnsum[:, 0:1]),
            inc="V", key=("dstt", t), waits=dsw)
        p.alias(("lnbstt", t), ("dstt", t))
        ln_emit(p, tn, "lnb", t, y2f)
        p.d("sync", lambda s, t=t, y2f=y2f: s.dma_start(
            out=tn.y_out[t * 128:(t + 1) * 128, :], in_=y2f()), q="DS",
            key=("yst", t), waits=[("lnbdone", t)])

    fin = [("yst", t) for t in range(NTT)]
    if DBG:
        fin += ["dbgsp", "dbgsp2"]
    p.e("sync", lambda s: None, waits=fin)


def ln_emit(p, tn, tag, t, yvf):
    p.e("vector", lambda v, yvf=yvf: v.scalar_tensor_tensor(
        out=tn.sqjunk[:], in0=yvf(), scalar=1.0, in1=yvf(),
        op0=ALU.mult, op1=ALU.mult, accum_out=tn.lnsq[:, 0:1]),
        inc="V", key=(tag + "sq", t), waits=[(tag + "stt", t)])

    def varf(v):
        v.tensor_scalar(out=tn.lnsum[:, 1:2], in0=tn.lnsum[:, 0:1],
                        scalar1=1.0 / H, scalar2=None, op0=ALU.mult)
        v.tensor_scalar(out=tn.lnsq[:, 1:2], in0=tn.lnsq[:, 0:1],
                        scalar1=1.0 / H, scalar2=None, op0=ALU.mult)
        v.tensor_tensor(out=tn.lnsum[:, 2:3], in0=tn.lnsum[:, 1:2],
                        in1=tn.lnsum[:, 1:2], op=ALU.mult)
        v.tensor_scalar(out=tn.lnsum[:, 3:4], in0=tn.lnsum[:, 1:2],
                        scalar1=1.0, scalar2=None, op0=ALU.mult)
        return v.tensor_tensor(out=tn.lnsq[:, 2:3], in0=tn.lnsq[:, 1:2],
                               in1=tn.lnsum[:, 2:3], op=ALU.subtract)
    vw = [(tag + "sq", t)]
    if DBG and tag == "lna" and t == 0:
        vw.append("dbg_t0b")
    p.e("vector", varf, inc="V", key=(tag + "var", t), waits=vw)

    def rstdf(s):
        s.activation(out=tn.lnsq[:, 3:4], in_=tn.lnsq[:, 2:3], func=AF.Ln,
                     bias=np.float32(LNEPS))
        return s.activation(out=tn.lnr[:, 0:1], in_=tn.lnsq[:, 3:4],
                            func=AF.Exp, scale=np.float32(-0.5))
    p.e("scalar", rstdf, inc="SC", key=(tag + "rstd", t), waits=[(tag + "var", t)])

    def normf(v, yvf=yvf):
        v.scalar_tensor_tensor(out=tn.lnr[:, 1:2], in0=tn.lnsum[:, 1:2],
                               scalar=-1.0, in1=tn.lnr[:, 0:1],
                               op0=ALU.mult, op1=ALU.mult)
        v.tensor_scalar(out=tn.lnsum[:, 3:4], in0=tn.lnsum[:, 1:2],
                        scalar1=1.0, scalar2=None, op0=ALU.mult)
        v.tensor_scalar(out=yvf(), in0=yvf(), scalar1=tn.lnr[:, 0:1],
                        scalar2=tn.lnr[:, 1:2], op0=ALU.mult, op1=ALU.add)
        return v.tensor_scalar(out=tn.lnsum[:, 3:4], in0=tn.lnsum[:, 1:2],
                               scalar1=1.0, scalar2=None,
                               op0=ALU.mult)
    p.e("vector", normf, inc="V", key=(tag + "done", t),
        waits=[(tag + "rstd", t)])



class TN:
    pass


def build_nc():
    nc = bass.Bass("TRN2", target_bir_lowering=False)
    tn = TN()
    D = tn.__dict__
    D["x_all"] = nc.dram_tensor("x_all", [S, H], F32, kind="ExternalInput").ap()
    D["x_own"] = nc.dram_tensor("x_own", [OWN, H], F32, kind="ExternalInput").ap()
    D["wqkv"] = nc.dram_tensor("wqkv", [H, 3 * H], F32, kind="ExternalInput").ap()
    D["wout"] = nc.dram_tensor("wout", [H, H], F32, kind="ExternalInput").ap()
    D["wg_in"] = nc.dram_tensor("wg_in", [FI, H], F32, kind="ExternalInput").ap()
    D["wu_in"] = nc.dram_tensor("wu_in", [FI, H], F32, kind="ExternalInput").ap()
    D["wd_in"] = nc.dram_tensor("wd_in", [H, FI], F32, kind="ExternalInput").ap()
    D["sintab"] = nc.dram_tensor("sintab", [128, 1024], BF16, kind="ExternalInput").ap()
    D["costab"] = nc.dram_tensor("costab", [128, 1024], BF16, kind="ExternalInput").ap()
    D["ident_in"] = nc.dram_tensor("ident_in", [128, 128], BF16, kind="ExternalInput").ap()
    D["masks_in"] = nc.dram_tensor("masks_in", [NQT * 8 * 128, QT], BF16,
                                   kind="ExternalInput").ap()
    D["cfgq"] = nc.dram_tensor("cfgq", [1, 8], I32, kind="ExternalInput").ap()
    D["y_out"] = nc.dram_tensor("y_out", [OWN, H], F32, kind="ExternalOutput").ap()
    for nm, shp, dt in [
        ("xbf_scr", [S, H], BF16), ("q_scr", [2, S, 128], BF16),
        ("k_scr", [2, S, 128], BF16), ("qown_scr", [2, OWN, 128], BF16),
        ("kperm_scr", [2, S, 128], BF16), ("attnT_scr", [NHP * 128, OWN], BF16),
        ("wqkv_bf", [H, 3 * H], BF16),
        ("wgq_scr", [FI, H], BF16), ("wuq_scr", [FI, H], BF16),
        ("wdq_scr", [H, FI], BF16), ("xqT_scr", [H, OWN], BF16),
        ("y1_scr", [OWN, H], F32),
    ]:
        D[nm] = nc.dram_tensor(nm, shp, dt, kind="Internal").ap()
    if DBG:
        D["d_qk"] = nc.dram_tensor("d_qk", [128, NCH * 256], BF16,
                                   kind="ExternalOutput").ap()
        D["d_vaug"] = nc.dram_tensor("d_vaug", [128, NCH * 130], BF16,
                                     kind="ExternalOutput").ap()
        D["d_attnT"] = nc.dram_tensor("d_attnT", [NHP * 128, OWN], BF16,
                                      kind="ExternalOutput").ap()
        D["d_y1"] = nc.dram_tensor("d_y1", [OWN, H], F32, kind="ExternalOutput").ap()
        D["d_wgq"] = nc.dram_tensor("d_wgq", [FI, H], BF16, kind="ExternalOutput").ap()
        D["d_wuq"] = nc.dram_tensor("d_wuq", [FI, H], BF16, kind="ExternalOutput").ap()
        D["d_wdq"] = nc.dram_tensor("d_wdq", [H, FI], BF16, kind="ExternalOutput").ap()
        D["d_xqT"] = nc.dram_tensor("d_xqT", [H, OWN], BF16, kind="ExternalOutput").ap()
        D["d_aT0"] = nc.dram_tensor("d_aT0", [128, 1024], BF16,
                                    kind="ExternalOutput").ap()
        D["d_ln0"] = nc.dram_tensor("d_ln0", [128, 10], F32,
                                    kind="ExternalOutput").ap()
        D["d_y1t0"] = nc.dram_tensor("d_y1t0", [128, H], F32,
                                     kind="ExternalOutput").ap()
        D["d_xres0"] = nc.dram_tensor("d_xres0", [128, H], F32,
                                      kind="ExternalOutput").ap()
        D["d_ops0"] = nc.dram_tensor("d_ops0", [128, H], F32,
                                     kind="ExternalOutput").ap()

    persist = []

    def sbp(name, shape, dt):
        g = nc.sbuf_tensor(name, shape, dt)
        persist.append(g)
        return g.__enter__()

    D["ident"] = sbp("identb", [128, 128], BF16)
    D["ones64"] = sbp("ones64b", [128, 64], BF16)
    D["cfg_sb"] = sbp("cfg_sb", [1, 8], I32)

    sems = {}
    for n in SEMS:
        g = nc.semaphore("sem_" + n)
        persist.append(g)
        sems[n] = g.__enter__()

    p12 = []

    def sb1(name, shape, dt):
        g = nc.sbuf_tensor(name, shape, dt)
        p12.append(g)
        return g.__enter__()

    def pb1(name, shape, dt):
        g = nc.psum_tensor(name, shape, dt)
        p12.append(g)
        return g.__enter__()

    D["sin_sb"] = sb1("sin_sb", [128, 1024], BF16)
    D["cos_sb"] = sb1("cos_sb", [128, 1024], BF16)
    D["xT"] = sb1("xT", [128, 8 * S], BF16)
    D["wqkv_sb"] = sb1("wqkv_sb", [128, 2 * 8 * 384], BF16)
    D["qk_tok"] = sb1("qk_tok", [128, NCH * 256], BF16)
    D["vaug"] = sb1("vaug", [128, 2 * NCH * 130], BF16)
    D["rope_t1"] = sb1("rope_t1", [128, 1024], BF16)
    D["rope_t2"] = sb1("rope_t2", [128, 1024], BF16)
    D["rope_t3"] = sb1("rope_t3", [128, 1024], BF16)
    D["qT_sb"] = sb1("qT_sb", [128, 2 * OWN], BF16)
    D["kT_sb"] = sb1("kT_sb", [128, 2 * S], BF16)
    D["pT_sb"] = sb1("pT_sb", [128, 4 * 1024], BF16)
    D["mask_sb"] = sb1("mask_sb", [128, 8 * QT], BF16)
    D["rs_sb"] = sb1("rs_sb", [128, QT], F32)
    D["rr_sb"] = sb1("rr_sb", [128, QT], F32)
    D["attnT_sb"] = sb1("attnT_sb", [128, OWN], BF16)
    D["vq_in"] = sb1("vq_in", [128, 4 * VQ_W], F32)
    D["vq_out"] = sb1("vq_out", [128, 2 * VQ_W], BF16)
    D["vq_t"] = sb1("vq_t", [128, VQ_W // 2], F32)
    D["vq_u1"] = sb1("vq_u1", [128, VQ_W // 2], F32)
    D["pjunk"] = sb1("pjunk", [128, 16], BF16)
    D["vq_u2"] = sb1("vq_u2", [128, VQ_W // 2], F32)
    D["vq_bm"] = sb1("vq_bm", [128, VQ_W // 16], F32)
    D["vq_bm2"] = sb1("vq_bm2", [128, VQ_W // 16], F32)
    D["qkv_ps"] = pb1("qkv_ps", [128, 1024], F32)
    D["sco_a"] = pb1("sco_a", [128, 1024], F32)
    D["sco_b"] = pb1("sco_b", [128, 1024], F32)
    D["pv_a"] = pb1("pv_a", [128, 512], F32)
    D["pv_b"] = pb1("pv_b", [128, 512], F32)

    p = P()
    record_p12_only = None

    record(p, tn)

    blk_g = nc.Block()
    blk = blk_g.__enter__()

    @blk.sync
    def _(sync):
        p.replay("sync", sync, sems, 1)

    @blk.vector
    def _(vector):
        p.replay("vector", vector, sems, 1)

    @blk.scalar
    def _(scalar):
        p.replay("scalar", scalar, sems, 1)

    @blk.tensor
    def _(tensor):
        p.replay("tensor", tensor, sems, 1)

    @blk.gpsimd
    def _(gpsimd):
        p.replay("gpsimd", gpsimd, sems, 1)

    blk_g.__exit__(None, None, None)

    for g in reversed(p12):
        g.__exit__(None, None, None)

    p34 = []

    def sb2(name, shape, dt):
        g = nc.sbuf_tensor(name, shape, dt)
        p34.append(g)
        return g.__enter__()

    def pb2(name, shape, dt):
        g = nc.psum_tensor(name, shape, dt)
        p34.append(g)
        return g.__enter__()

    D["wout_sb"] = sb2("wout_sb", [128, 8 * 1024], BF16)
    D["lnsum"] = sb2("lnsum", [128, 4], F32)
    D["lnsq"] = sb2("lnsq", [128, 4], F32)
    D["lnr"] = sb2("lnr", [128, 2], F32)
    D["wgT"] = sb2("wgT", [128, 8 * FI], BF16)
    D["wuT"] = sb2("wuT", [128, 8 * FI], BF16)
    D["wdT"] = sb2("wdT", [128, 22 * H], BF16)
    D["aT_t"] = sb2("aT_t", [128, 1024], BF16)
    D["xres_t"] = sb2("xres_t", [128, H], F32)
    D["y1_t"] = sb2("y1_t", [128, H], F32)
    D["y14_t"] = sb2("y14_t", [128, H], F32)
    D["sqjunk"] = sb2("sqjunk", [128, H], BF16)
    D["xq_t"] = sb2("xq_t", [128, H], BF16)
    D["xbm"] = sb2("xbm", [128, 64], F32)
    D["xbm2"] = sb2("xbm2", [128, 64], F32)
    D["xqT_t"] = sb2("xqT_t", [128, 1024], BF16)
    D["xqT4_t"] = sb2("xqT4_t", [128, 2 * 1024], BF16)
    D["hg_t"] = sb2("hg_t", [128, FI], F32)
    D["hu_t"] = sb2("hu_t", [128, FI], F32)
    D["hq_t"] = sb2("hq_t", [128, FI], BF16)
    D["hbm"] = sb2("hbm", [128, 176], F32)
    D["hbm2"] = sb2("hbm2", [128, 176], F32)
    D["hqT_t"] = sb2("hqT_t", [128, 22 * 128], BF16)
    D["qs1f"] = tn.hqT_t[:].bitcast(F32)
    D["hq_u2"] = tn.wout_sb[:].bitcast(F32)
    D["qs1"] = tn.hg_t
    D["qs2"] = tn.qs1f
    D["qs3"] = tn.hu_t
    D["op_ps"] = pb2("op_ps", [128, 1024], F32)
    D["xqT_ps"] = pb2("xqT_ps", [128, 1024], BF16)
    D["g_ps"] = pb2("g_ps", [128, 512], F32)
    D["u_ps"] = pb2("u_ps", [128, 512], F32)
    D["d_ps"] = pb2("d_ps", [128, 1024], F32)
    D["hqT_ps"] = pb2("hqT_ps", [128, 1024], BF16)
    D["op_ps_g"] = tn.op_ps[:, 0:512]
    D["op_ps_u"] = tn.op_ps[:, 512:1024]

    blk2_g = nc.Block("p34")
    blk2 = blk2_g.__enter__()

    @blk2.sync
    def _(sync):
        p.replay("sync", sync, sems, 2)

    @blk2.vector
    def _(vector):
        p.replay("vector", vector, sems, 2)

    @blk2.scalar
    def _(scalar):
        p.replay("scalar", scalar, sems, 2)

    @blk2.tensor
    def _(tensor):
        p.replay("tensor", tensor, sems, 2)

    @blk2.gpsimd
    def _(gpsimd):
        p.replay("gpsimd", gpsimd, sems, 2)

    blk2_g.__exit__(None, None, None)

    for g in reversed(p34):
        g.__exit__(None, None, None)
    for g in reversed(persist):
        g.__exit__(None, None, None)
    return nc



def host_prep(inputs):
    x = np.asarray(inputs["L0_attn_hidden_states"])
    w_qkv = np.asarray(inputs["L0_attn_w_qkv"])
    w_out = np.asarray(inputs["L0_attn_w_out"])
    w_gate = np.asarray(inputs["L0_ffn_w_gate"])
    w_up = np.asarray(inputs["L0_ffn_w_up"])
    w_down = np.asarray(inputs["L0_ffn_w_down"])

    evens = list(range(0, 32, 2))
    odds = list(range(1, 32, 2))
    perm = []
    for hp in range(NHP):
        ha, hb = 2 * hp, 2 * hp + 1

        def qcols(h, dperm):
            base = 384 * (h // 2) + 64 * (h % 2)
            return [base + d for d in dperm]
        qp = (qcols(ha, evens) + qcols(hb, evens) + qcols(ha, odds)
              + qcols(hb, odds) + qcols(ha, range(32, 48)) + qcols(hb, range(32, 48))
              + qcols(ha, range(48, 64)) + qcols(hb, range(48, 64)))
        kp = [c + 128 for c in qp]
        vp = ([384 * (ha // 2) + 64 * (ha % 2) + 256 + d for d in range(64)]
              + [384 * (hb // 2) + 64 * (hb % 2) + 256 + d for d in range(64)])
        perm += qp + kp + vp
    wqkv_r = np.ascontiguousarray(w_qkv[:, perm])

    inv = 1.0 / 10000.0 ** (np.arange(0, 32, 2, dtype=np.float64) / 32.0)
    ang = np.arange(S, dtype=np.float64)[:, None] * inv[None, :]

    def tab(a):
        a3 = a.reshape(NCH, 128, 16)
        out = np.zeros((128, NCH * 32), dtype=np.float32)
        for c in range(NCH):
            out[:, c * 32:c * 32 + 16] = a3[c]
            out[:, c * 32 + 16:c * 32 + 32] = a3[c]
        return out.astype(BF)
    sintab = tab(np.sin(ang).astype(np.float32))
    costab = tab(np.cos(ang).astype(np.float32))
    ident = np.eye(128, dtype=np.float32).astype(BF)

    in_maps = []
    for c in range(8):
        b, f = c // 2, c % 2
        gq = FOLD_Q[f]
        xb = np.ascontiguousarray(x[b])
        x_own = np.ascontiguousarray(
            np.concatenate([xb[g * QT:(g + 1) * QT] for g in gq], axis=0))
        cfg = np.zeros((1, 8), dtype=np.int32)
        cfg[0, :4] = gq
        masks = np.zeros((NQT, 8, 128, QT), dtype=np.float32)
        for j in range(NQT):
            Dj = DEPTH[j]
            q0 = gq[j] * QT
            for m in range(8):
                i = Dj - 8 + m
                kg = i * 128 + np.arange(128)[:, None]
                qg = q0 + np.arange(QT)[None, :]
                masks[j, m] = (kg <= qg).astype(np.float32)
        in_maps.append({
            "x_all": xb, "x_own": x_own, "wqkv": wqkv_r,
            "wout": np.ascontiguousarray(w_out),
            "wg_in": np.ascontiguousarray(w_gate),
            "wu_in": np.ascontiguousarray(w_up),
            "wd_in": np.ascontiguousarray(w_down),
            "sintab": sintab, "costab": costab, "ident_in": ident,
            "masks_in": masks.reshape(NQT * 8 * 128, QT).astype(BF),
            "cfgq": cfg,
        })
    return in_maps


_NC_CACHE = [None]
_LAST = [None]


def kernel(**inputs):
    in_maps = host_prep(inputs)
    if _NC_CACHE[0] is None:
        _NC_CACHE[0] = build_nc()
    res = run_bass_kernel_spmd(_NC_CACHE[0], in_maps, core_ids=list(range(8)))
    _LAST[0] = res
    out = np.zeros((4, S, H), dtype=np.float32)
    for c in range(8):
        b, f = c // 2, c % 2
        for j, g in enumerate(FOLD_Q[f]):
            out[b, g * QT:(g + 1) * QT] = res.results[c]["y_out"][j * QT:(j + 1) * QT]
    return out


# version 09
# speedup vs baseline: 1.0514x; 1.0514x over previous
import os
import numpy as np
import ml_dtypes
import concourse.bass as bass
import concourse.mybir as mybir
from concourse.bass_utils import run_bass_kernel_spmd

F32 = mybir.dt.float32
BF16 = mybir.dt.bfloat16
I32 = mybir.dt.int32
AF = mybir.ActivationFunctionType
ALU = mybir.AluOpType
BF = ml_dtypes.bfloat16

S, H, NHP, FI = 4096, 1024, 8, 2816
NCH, OWN, NQT, QT = 32, 2048, 4, 512
NG = NCH // 2
DEPTH = (8, 16, 24, 32)
TOT_SUPERS = sum(d // 2 for d in DEPTH)
CEXP = 3.0
LNEPS = 1e-5
MID = (0.25, 0.75, 1.25, 1.75, 2.5, 3.5, 5.0)
NTT = 16
NS = 6
DBG = 0

VQ = ([("wg", i * 128, 0, H) for i in range(22)] +
      [("wu", i * 128, 0, H) for i in range(22)] +
      [("wd", (i // 2) * 128, (i % 2) * 1408, 1408) for i in range(16)])
NVQ = len(VQ)
VQ_W = 1408
ENGS = ("sync", "vector", "scalar", "tensor", "gpsimd")
DMA_POOL = {"DS": 12, "DG": 6, "DA": 4, "DW": 8}
SEMS = (("T", "V", "SC", "G")
        + tuple(f"{q}{i}" for q, n in DMA_POOL.items() for i in range(n)))
FOLD_Q = [[0, 3, 4, 7], [1, 2, 5, 6]]


class P:
    def __init__(self):
        self.cnt = {n: 0 for n in SEMS}
        self.M = {}
        self.stream = {e: [] for e in ENGS}
        self.split = {}
        self.qrot = {}
        self.curgrp = {}
        self.qreg = None

    def e(self, eng, fn, inc=None, key=None, waits=()):
        sname = None
        n = 0
        if inc is not None:
            sname, n = (inc, 1) if isinstance(inc, str) else inc
            self.cnt[sname] += n
        if key is not None:
            assert key not in self.M, ("dup milestone", key)
            self.M[key] = (sname, self.cnt[sname])
        self.stream[eng].append((tuple(waits), fn, sname, n))

    def d(self, eng, fn, q="DS", key=None, waits=(), samegroup=False):
        if not samegroup:
            i = self.qrot[q] = (self.qrot.get(q, -1) + 1) % DMA_POOL[q]
            s = f"{q}{i}"
            self.curgrp[q] = s
            prev = self.cnt[s]
            waits = tuple(waits) + ((("_raw", s, prev),) if prev > 0 else ())
        else:
            s = self.curgrp[q]
        self.e(eng, fn, inc=(s, 16), key=key, waits=waits)

    def alias(self, newkey, oldkey):
        assert newkey not in self.M
        self.M[newkey] = self.M[oldkey]

    def mark_split(self):
        for e in ENGS:
            self.split[e] = len(self.stream[e])

    def barrier_waits(self):
        return [("_raw", s, self.cnt[s]) for s in SEMS]

    def replay(self, engname, eng, sems, phase):
        lo = 0 if phase == 1 else self.split[engname]
        hi = self.split[engname] if phase == 1 else len(self.stream[engname])
        for waits, fn, sname, n in self.stream[engname][lo:hi]:
            for w in waits:
                if isinstance(w, tuple) and len(w) == 3 and w[0] == "_raw":
                    ws, val = w[1], w[2]
                else:
                    ws, val = self.M[w]
                if val > 0:
                    eng.wait_ge(sems[ws], val)
            inst = fn(eng)
            if sname is not None:
                assert inst is not None, (engname, sname)
                inst.then_inc(sems[sname], n)


def quant_stairs(eng, xin, out, width, rinv, scl, t_t, sgn_t, u1):
    nb = width // 16
    rb = rinv.unsqueeze(2).broadcast_to([128, nb, 16])
    sb_ = scl.unsqueeze(2).broadcast_to([128, nb, 16])
    tt = t_t[:, 0:width]
    sg = sgn_t[:, 0:width]
    uu1 = u1
    eng.tensor_tensor(out=tt.rearrange("p (b k) -> p b k", k=16),
                      in0=xin.rearrange("p (b k) -> p b k", k=16),
                      in1=rb, op=ALU.mult)
    eng.tensor_scalar(out=sg, in0=tt, scalar1=0.0, scalar2=0.5,
                      op0=ALU.is_ge, op1=ALU.subtract)
    eng.scalar_tensor_tensor(out=tt, in0=tt, scalar=2.0, in1=sg,
                             op0=ALU.mult, op1=ALU.mult)
    eng.tensor_scalar(out=uu1, in0=tt, scalar1=MID[0], scalar2=None,
                      op0=ALU.is_gt)
    for m in (MID[1], MID[2], MID[3]):
        eng.scalar_tensor_tensor(out=uu1, in0=tt, scalar=float(m), in1=uu1,
                                 op0=ALU.is_gt, op1=ALU.add)
    eng.tensor_scalar(out=out, in0=tt, scalar1=MID[4], scalar2=None,
                      op0=ALU.is_gt)
    eng.scalar_tensor_tensor(out=out, in0=tt, scalar=MID[5], in1=out,
                             op0=ALU.is_gt, op1=ALU.add)
    eng.scalar_tensor_tensor(out=out, in0=tt, scalar=MID[6], in1=out,
                             op0=ALU.is_gt, op1=ALU.add)
    eng.scalar_tensor_tensor(out=uu1, in0=uu1, scalar=0.5, in1=out,
                             op0=ALU.mult, op1=ALU.add)
    eng.scalar_tensor_tensor(out=uu1, in0=tt, scalar=MID[6], in1=uu1,
                             op0=ALU.is_gt, op1=ALU.add)
    eng.tensor_tensor(out=uu1, in0=uu1, in1=sg, op=ALU.mult)
    return eng.scalar_tensor_tensor(
        out=out.rearrange("p (b k) -> p b k", k=16),
        in0=uu1.rearrange("p (b k) -> p b k", k=16),
        scalar=2.0, in1=sb_, op0=ALU.mult, op1=ALU.mult)


MID2 = tuple(float(m) * float(m) for m in MID)


def quant_prep(eng, xin, width, bm, bm2, junk):
    nb = width // 16
    eng.tensor_reduce(out=bm[:, 0:nb],
                      in_=xin.rearrange("p (b k) -> p b k", k=16),
                      axis=mybir.AxisListType.X, op=ALU.max,
                      apply_absolute_value=True)
    eng.tensor_scalar(out=junk[:, 0:nb], in0=xin[:, 0:nb], scalar1=1.0,
                      scalar2=None, op0=ALU.mult)
    eng.tensor_scalar(out=bm[:, 0:nb], in0=bm[:, 0:nb], scalar1=1.0 / 6.0,
                      scalar2=1e-12, op0=ALU.mult, op1=ALU.max)
    eng.tensor_scalar(out=junk[:, 0:nb], in0=xin[:, 0:nb], scalar1=1.0,
                      scalar2=None, op0=ALU.mult)
    eng.reciprocal(out=bm2[:, 0:nb], in_=bm[:, 0:nb])
    return eng.tensor_scalar(out=junk[:, 0:nb], in0=xin[:, 0:nb], scalar1=1.0,
                             scalar2=None, op0=ALU.mult)



def record(p, tn):
    rec_sync_p0(p, tn)
    rec_gp_p12(p, tn)
    rec_tensor_p12(p, tn)
    rec_vector_p12(p, tn)
    rec_scalar_p12(p, tn)
    rec_sync_p12(p, tn)
    p.mark_split()
    rec_p34(p, tn)


def rec_sync_p0(p, tn):
    p.d("sync", lambda s: s.dma_start(out=tn.cfg_sb[:], in_=tn.cfgq), key="cfg")
    p.d("sync", lambda s: s.dma_start(out=tn.ident[:], in_=tn.ident_in))
    p.d("sync", lambda s: s.dma_start(out=tn.sin_sb[:], in_=tn.sintab),
        samegroup=True)
    p.d("sync", lambda s: s.dma_start(out=tn.cos_sb[:], in_=tn.costab),
        samegroup=True, key="tabs")

    def ld(s):
        p.qreg = [s.value_load(tn.cfg_sb[0:1, j:j + 1]) for j in range(NQT)]
        return None
    p.e("sync", ld, waits=["cfg"])


def rec_gp_p12(p, tn):
    def vq_load(i):
        if i >= NVQ:
            return
        wname, r0, c0, width = VQ[i]
        src_ = {"wg": tn.wg_in, "wu": tn.wu_in, "wd": tn.wd_in}[wname]
        b = i % 4
        p.d("gpsimd", lambda g, src_=src_, r0=r0, c0=c0, width=width, b=b:
            g.dma_start(out=tn.vq_in[:, b * VQ_W:b * VQ_W + width],
                        in_=src_[r0:r0 + 128, c0:c0 + width]), q="DW",
            key=("vqin", i))

    vq_done = [0]

    def vq_io(nmax):
        done = 0
        while vq_done[0] < NVQ and done < nmax:
            i = vq_done[0]
            wname, r0, c0, width = VQ[i]
            dst = {"wg": tn.wgq_scr, "wu": tn.wuq_scr, "wd": tn.wdq_scr}[wname]
            sw = [("vqrs", i + 1)] if i + 1 < NVQ else ["vqflush"]
            p.d("gpsimd", lambda g, dst=dst, r0=r0, c0=c0, width=width, i=i:
                g.dma_start(out=dst[r0:r0 + 128, c0:c0 + width],
                            in_=tn.vq_out[:, (i % 2) * VQ_W:
                                          (i % 2) * VQ_W + width]), q="DW",
                key=("vqsp", i), waits=sw)
            vq_load(i + 4)
            vq_done[0] += 1
            done += 1

    vq_load(0)
    vq_load(1)
    for i in range(4):
        p.d("gpsimd", lambda g, i=i: g.dma_start(
            out=tn.xbf_scr[i * 1024:(i + 1) * 1024, :],
            in_=tn.x_all[i * 1024:(i + 1) * 1024, :]), q="DG",
            samegroup=(i > 0), key="xcast" if i == 3 else None)
    vq_load(2)
    vq_load(3)

    p.d("gpsimd", lambda g: g.dma_start(out=tn.wqkv_bf, in_=tn.wqkv),
        q="DG", key="wqkvcast")
    for hp in range(NHP):
        vq_io(7)
    vq_io(NVQ)


def rec_tensor_p12(p, tn):
    for hp in range(NHP):
        att = attention_qt_items(p, tn, hp - 1) if hp >= 1 else [[], [], [], []]
        for j4 in range(NQT):
            for c in range(8 * j4, 8 * j4 + 8):
                qkv_chunk_item(p, tn, hp, c)()
            for fn in att[j4]:
                fn()
    for blk in attention_qt_items(p, tn, NHP - 1):
        for fn in blk:
            fn()


def qkv_chunk_item(p, tn, hp, c):
    buf = hp % 2
    waits = []
    if c == 0:
        waits += ["xT", ("wqkv", hp)]
        if hp >= 1:
            waits.append(("qout", hp - 1))
            waits.append(("qkout", hp - 1))
        if hp >= 2:
            waits.append(("pvs", hp - 2, TOT_SUPERS - 1))
    elif c == 1:
        if hp >= 1:
            waits.append(("qevict", hp - 1, NCH - 1))
    else:
        waits.append(("qevict", hp, c - 2))

    def qmm(te, hp=hp, c=c, buf=buf):
        last = None
        h0 = (c % 2) * 512
        for hc in range(8):
            last = te.matmul(
                tn.qkv_ps[:, h0:h0 + 384],
                tn.xT[:, hc * S + c * 128: hc * S + (c + 1) * 128],
                tn.wqkv_sb[:, buf * 3072 + hc * 384:
                           buf * 3072 + (hc + 1) * 384],
                start=(hc == 0), stop=(hc == 7))
        return last
    return lambda: p.e("tensor", qmm, inc="T", key=("qkvmm", hp, c),
                       waits=waits)


def attention_qt_items(p, tn, hp):
    buf = hp % 2
    blocks = []
    sup = 0
    for j in range(NQT):
        items = []
        D = DEPTH[j]
        for s in range(D // 2):
            waits = []
            if sup == 0:
                waits.append(("qT", hp))
                waits.append(("qkT", hp))
            if sup > 0:
                waits.append(("exp", hp, sup - 1, 1))

            def qk(te, hp=hp, j=j, s=s, buf=buf):
                lastB = None
                for ii in range(2):
                    i = 2 * s + ii
                    for head in range(2):
                        dst = tn.sco_a if head == 0 else tn.sco_b
                        lm = te.matmul(
                            dst[:, ii * 512:(ii + 1) * 512],
                            tn.kT_sb[head * 64:(head + 1) * 64,
                                     buf * S + i * 128: buf * S + (i + 1) * 128],
                            tn.qT_sb[head * 64:(head + 1) * 64,
                                     buf * OWN + j * QT: buf * OWN + (j + 1) * QT],
                            start=True, stop=True, tile_position=(head * 64, 0))
                        if head == 1:
                            lastB = lm
                return lastB
            items.append(lambda qk=qk, hp=hp, sup=sup, waits=tuple(waits):
                         p.e("tensor", qk, inc="T", key=("qk", hp, sup),
                             waits=waits))
            if s > 0:
                items.append(pv_super_item(p, tn, hp, j, s - 1, sup - 1))
            sup += 1
        items.append(pv_super_item(p, tn, hp, j, D // 2 - 1, sup - 1))
        items.append(lambda p=p, hp=hp, j=j, sup=sup:
                     p.alias(("pvq", hp, j), ("pvs", hp, sup - 1)))
        blocks.append(items)
    return blocks


def pv_super_item(p, tn, hp, j, s, sup):
    D = DEPTH[j]
    buf = hp % 2
    waits = [("exp", hp, sup, 0), ("exp", hp, sup, 1), ("maskv", hp, sup)]
    if s == 0:
        if j > 0:
            waits.append(("normq", hp, j - 1))
        elif hp > 0:
            waits.append(("normq", hp - 1, NQT - 1))
    pbuf = sup % 2

    def pv(te, hp=hp, j=j, s=s, pbuf=pbuf, buf=buf, D=D):
        last = None
        for ii in range(2):
            i = 2 * s + ii
            for head in range(2):
                pvp = tn.pv_a if head == 0 else tn.pv_b
                pt = tn.pT_sb[:, (head * 2 + pbuf) * 1024 + ii * 512:
                              (head * 2 + pbuf) * 1024 + (ii + 1) * 512]
                te.matmul(pvp[0:64, :],
                          tn.vaug[:, buf * NCH * 130 + i * 130 + head * 65:
                                  buf * NCH * 130 + i * 130 + head * 65 + 64],
                          pt, start=(i == 0), stop=(i == D - 1),
                          tile_position=(0, 0), skip_group_check=True)
                last = te.matmul(pvp[64:128, :], tn.ones64[:], pt,
                                 start=(i == 0), stop=(i == D - 1),
                                 tile_position=(0, 64), skip_group_check=True)
        return last
    return lambda: p.e("tensor", pv, inc="T", key=("pvs", hp, sup),
                       waits=waits)


def rec_vector_p12(p, tn):
    vq_next = [0]

    def do_vq(nmax):
        done = 0
        while vq_next[0] < NVQ and done < nmax:
            i = vq_next[0]
            wname, r0, c0, width = VQ[i]
            b = i % 4

            def prep(v, i=i, width=width, b=b):
                return quant_prep(v, tn.vq_in[:, b * VQ_W:b * VQ_W + width],
                                  width, tn.vq_bm, tn.vq_bm2, tn.vq_t)
            p.e("vector", prep, inc="V", key=("vqrs", i), waits=[("vqin", i)])

            def qf(v, i=i, width=width, b=b):
                hw = width // 2
                last = None
                for off in (0, hw):
                    xin = tn.vq_in[:, b * VQ_W + off:b * VQ_W + off + hw]
                    last = quant_stairs(
                        v, xin,
                        tn.vq_out[:, (i % 2) * VQ_W + off:
                                  (i % 2) * VQ_W + off + hw],
                        hw, tn.vq_bm2[:, off // 16:(off + hw) // 16],
                        tn.vq_bm[:, off // 16:(off + hw) // 16],
                        tn.vq_t, tn.vq_u2, tn.vq_u1[:, 0:hw])
                return last
            qw = [("vqsp", i - 2)] if i >= 2 else []
            p.e("vector", qf, inc="V", key=("vqq", i), waits=qw)
            vq_next[0] += 1
            done += 1
    p.e("vector", lambda v: v.memset(tn.ones64[:], 1.0), inc="V", key="ones")
    do_vq(4)
    VQ_PER_QT = (1, 1, 2, 3)
    for hp in range(NHP):
        buf = hp % 2

        def ev(v, hp=hp, c=None, buf=buf, first=None):
            if first:
                v.memset(tn.vaug[:, buf * NCH * 130:(buf + 1) * NCH * 130], 1.0)
            h0 = (c % 2) * 512
            qk = tn.qk_tok[:]
            v.tensor_copy(
                out=qk.rearrange("p (c r) -> p c r", r=256)[:, c:c + 1, :],
                in_=tn.qkv_ps[:].rearrange("p (c r) -> p c r", r=512)[
                    :, c % 2:c % 2 + 1, 0:256])
            return v.tensor_copy(
                out=tn.vaug[:, buf * NCH * 130:(buf + 1) * NCH * 130].rearrange(
                    "p (c h sx) -> p c h sx", h=2, sx=65)[
                    :, c:c + 1, :, 0:64],
                in_=tn.qkv_ps[:].rearrange("p (c r) -> p c r", r=512)[
                    :, c % 2:c % 2 + 1, 256:384].rearrange(
                        "p c (h d) -> p c h d", d=64))

        def rope(v, half=0):
            qk = tn.qk_tok[:].rearrange("p (c qk r) -> p c qk r", qk=2, r=128)
            c0, c1 = half * 16, (half + 1) * 16
            Ev = qk[:, c0:c1, :, 0:32]
            Ov = qk[:, c0:c1, :, 32:64]
            cosv = tn.cos_sb[:].rearrange("p (c w) -> p c w", w=32)[
                :, c0:c1].unsqueeze(2).broadcast_to([128, 16, 2, 32])
            sinv = tn.sin_sb[:].rearrange("p (c w) -> p c w", w=32)[
                :, c0:c1].unsqueeze(2).broadcast_to([128, 16, 2, 32])
            t1 = tn.rope_t1[:].rearrange("p (c qk r) -> p c qk r", qk=2, r=32)
            t2 = tn.rope_t2[:].rearrange("p (c qk r) -> p c qk r", qk=2, r=32)
            t3 = tn.rope_t3[:].rearrange("p (c qk r) -> p c qk r", qk=2, r=32)
            v.tensor_tensor(out=t1, in0=Ev, in1=cosv, op=ALU.mult)
            v.tensor_tensor(out=t2, in0=Ov, in1=sinv, op=ALU.mult)
            v.tensor_tensor(out=t1, in0=t1, in1=t2, op=ALU.subtract)
            v.tensor_tensor(out=t2, in0=Ev, in1=sinv, op=ALU.mult)
            v.tensor_tensor(out=t3, in0=Ov, in1=cosv, op=ALU.mult)
            v.tensor_tensor(out=Ov, in0=t2, in1=t3, op=ALU.add)
            return v.tensor_copy(out=Ev, in_=t1)

        def rope_end(v):
            rope(v, 1)
            return v.tensor_scalar(out=tn.rope_t1[:, 0:16],
                                   in0=tn.rope_t2[:, 0:16], scalar1=1.0,
                                   scalar2=None, op0=ALU.mult)

        for j4 in range(NQT):
            for c in range(8 * j4, 8 * j4 + 8):
                waits = [("qkvmm", hp, c)]
                if c == 0 and hp >= 2:
                    waits.append(("pvs", hp - 2, TOT_SUPERS - 1))
                p.e("vector",
                    (lambda v, hp=hp, c=c, buf=buf, first=(c == 0):
                     ev(v, hp, c, buf, first)),
                    inc="V", key=("qevict", hp, c), waits=waits)
            if j4 == 3:
                p.e("vector", lambda v: rope(v, 0),
                    waits=["tabs", ("qevict", hp, NCH - 1)])
                p.e("vector", rope_end, inc="V", key=("rope", hp))
            if hp >= 1:
                attv_qt(p, tn, hp - 1, j4)
            do_vq(VQ_PER_QT[j4])
        if hp >= 1:
            p.alias(("norm", hp - 1), ("normq", hp - 1, NQT - 1))
    for j4 in range(NQT):
        attv_qt(p, tn, NHP - 1, j4)
    p.alias(("norm", NHP - 1), ("normq", NHP - 1, NQT - 1))
    do_vq(NVQ)
    p.e("vector", lambda v: v.tensor_scalar(
        out=tn.rope_t1[:, 0:16], in0=tn.rope_t2[:, 0:16], scalar1=1.0,
        scalar2=None, op0=ALU.mult), inc="V", key="vqflush")


def attv_qt(p, tn, hp, j):
    buf = hp % 2
    sup = sum(DEPTH[jj] // 2 for jj in range(j))
    D = DEPTH[j]
    for s in range(D // 2):
        pbuf = sup % 2
        masked = [2 * s + ii for ii in range(2) if 2 * s + ii >= D - 8]
        if masked:
            def mk(v, s=s, pbuf=pbuf, masked=tuple(masked), D=D):
                last = None
                for i in masked:
                    m = i - (D - 8)
                    ii = i - 2 * s
                    for head in range(2):
                        pt = tn.pT_sb[:, (head * 2 + pbuf) * 1024 + ii * 512:
                                      (head * 2 + pbuf) * 1024 + (ii + 1) * 512]
                        last = v.tensor_tensor(
                            out=pt, in0=pt,
                            in1=tn.mask_sb[:, m * QT:(m + 1) * QT],
                            op=ALU.mult)
                return last
            p.e("vector", mk, inc="V", key=("maskv", hp, sup),
                waits=[("exp", hp, sup, 0), ("exp", hp, sup, 1),
                       ("mload", hp, j)])
        else:
            p.M[("maskv", hp, sup)] = ("V", p.cnt["V"])
        sup += 1

    def sums(v):
        v.tensor_copy(out=tn.rs_sb[0:64, :], in_=tn.pv_a[64:128, :])
        return v.tensor_copy(out=tn.rs_sb[64:128, :], in_=tn.pv_b[64:128, :])
    p.e("vector", sums, inc="V", key=("sums", hp, j), waits=[("pvq", hp, j)])

    nw = [("recip", hp, j)]
    if j == 0 and hp >= 1:
        nw.append(("attnsp", hp - 1))

    def norm(v, j=j):
        v.tensor_tensor(
            out=tn.attnT_sb[0:64, j * QT:(j + 1) * QT],
            in0=tn.pv_a[0:64, :], in1=tn.rr_sb[0:64, :], op=ALU.mult)
        v.tensor_tensor(
            out=tn.attnT_sb[64:128, j * QT:(j + 1) * QT],
            in0=tn.pv_b[0:64, :], in1=tn.rr_sb[64:128, :], op=ALU.mult)
        return v.tensor_scalar(out=tn.rope_t3[:, 0:16],
                               in0=tn.rope_t2[:, 0:16], scalar1=1.0,
                               scalar2=None, op0=ALU.mult)
    p.e("vector", norm, inc="V", key=("normq", hp, j), waits=nw)


def rec_scalar_p12(p, tn):
    def wqkv_ld(hp):
        if hp >= NHP:
            return
        buf = hp % 2
        w = ["wqkvcast"] if hp < 2 else [("qkvmm", hp - 2, NCH - 1)]
        p.d("scalar", lambda s, hp=hp, buf=buf: s.dma_start(
            out=tn.wqkv_sb[:, buf * 3072:(buf + 1) * 3072].rearrange(
                "p (hc c) -> p hc c", hc=8),
            in_=tn.wqkv_bf.rearrange("(hc pp) c -> pp hc c", pp=128)[
                :, :, hp * 384:(hp + 1) * 384]), q="DA",
            key=("wqkv", hp), waits=w)

    wqkv_ld(0)
    wqkv_ld(1)
    for hp in range(NHP):
        wqkv_ld(hp + 2)
        sup = 0
        for j in range(NQT):
            D = DEPTH[j]
            mwaits = []
            if hp * NQT + j >= 1:
                jj = hp * NQT + j - 1
                mwaits.append(("normq", jj // NQT, jj % NQT))
            p.d("scalar", lambda s, j=j: s.dma_start(
                out=tn.mask_sb[:].rearrange("p (m q) -> p m q", m=8),
                in_=tn.masks_in.rearrange("(j m pp) q -> pp j m q",
                                          m=8, pp=128)[:, j]),
                q="DA", key=("mload", hp, j), waits=mwaits)
            for s in range(D // 2):
                pbuf = sup % 2
                for head in range(2):
                    waits = [("qk", hp, sup)]
                    if sup >= 2:
                        waits.append(("pvs", hp, sup - 2))

                    def ex(sc, head=head, pbuf=pbuf):
                        src = tn.sco_a if head == 0 else tn.sco_b
                        return sc.activation(
                            out=tn.pT_sb[:, (head * 2 + pbuf) * 1024:
                                         (head * 2 + pbuf + 1) * 1024],
                            in_=src[:], func=AF.Exp,
                            bias=np.float32(-CEXP), scale=np.float32(0.125))
                    p.e("scalar", ex, inc="SC", key=("exp", hp, sup, head),
                        waits=waits)
                sup += 1

            def recip(sc):
                sc.activation(out=tn.rr_sb[:], in_=tn.rs_sb[:], func=AF.Ln)
                return sc.activation(out=tn.rr_sb[:], in_=tn.rr_sb[:],
                                     func=AF.Exp, scale=np.float32(-1.0))
            p.e("scalar", recip, inc="SC", key=("recip", hp, j),
                waits=[("sums", hp, j)])


def rec_sync_p12(p, tn):
    for hc in range(8):
        p.d("sync", lambda s, hc=hc: s.dma_start_transpose(
            tn.xT[:, hc * S:(hc + 1) * S],
            tn.xbf_scr[:, hc * 128:(hc + 1) * 128]), q="DS",
            waits=["xcast", "tabs"] if hc == 0 else [], samegroup=(hc > 0),
            key="xT" if hc == 7 else None)
    for hp in range(NHP):
        buf = hp % 2
        p.d("sync", lambda s, buf=buf: s.dma_start(
            out=tn.q_scr[buf].rearrange("(c pp) d -> pp c d", pp=128),
            in_=tn.qk_tok[:].rearrange("p (c r) -> p c r", r=256)[:, :, 0:128]),
            q="DS", waits=[("rope", hp)], key=("qout", hp))
        p.d("sync", lambda s, buf=buf: s.dma_start(
            out=tn.k_scr[buf].rearrange("(c pp) d -> pp c d", pp=128),
            in_=tn.qk_tok[:].rearrange("p (c r) -> p c r", r=256)[:, :, 128:256]),
            q="DS", key=("qkout", hp))
        first = True
        for j in range(NQT):
            for hh in range(2):
                p.d("sync", lambda s, j=j, buf=buf, hh=hh: s.dma_start(
                    out=tn.qown_scr[buf][j * QT:(j + 1) * QT,
                                         hh * 64:(hh + 1) * 64],
                    in_=tn.q_scr[buf][bass.ts(p.qreg[j], QT), :].rearrange(
                        "r (g h k) -> r h g k", g=4, h=2)[:, hh]), q="DS",
                    waits=[("qout", hp)] if first else [],
                    samegroup=not first,
                    key=("qgather", hp) if (j == NQT - 1 and hh == 1) else None)
                first = False
        for hh in range(2):
            p.d("sync", lambda s, buf=buf, hh=hh: s.dma_start(
                out=tn.kperm_scr[buf][:, hh * 64:(hh + 1) * 64],
                in_=tn.k_scr[buf].rearrange(
                    "r (g h k) -> r h g k", g=4, h=2)[:, hh]), q="DS",
                waits=[("qkout", hp)] if hh == 0 else [], samegroup=(hh == 1),
                key=("qkperm", hp) if hh == 1 else None)
        twaits = [("qgather", hp)]
        if hp >= 2:
            twaits.append(("pvs", hp - 2, TOT_SUPERS - 1))
        p.d("sync", lambda s, buf=buf: s.dma_start_transpose(
            tn.qT_sb[:, buf * OWN:(buf + 1) * OWN], tn.qown_scr[buf]),
            q="DS", waits=twaits, key=("qT", hp))
        p.d("sync", lambda s, buf=buf: s.dma_start_transpose(
            tn.kT_sb[:, buf * S:(buf + 1) * S], tn.kperm_scr[buf]),
            q="DS", waits=[("qkperm", hp)], key=("qkT", hp))
        p.d("sync", lambda s, hp=hp: s.dma_start(
            out=tn.attnT_scr[hp * 128:(hp + 1) * 128, :],
            in_=tn.attnT_sb[:]), q="DS", key=("attnsp", hp),
            waits=[("norm", hp)])
    if DBG:
        p.d("sync", lambda s: s.dma_start(out=tn.d_attnT, in_=tn.attnT_scr),
            q="DS", waits=[("attnsp", NHP - 1)])
        p.d("sync", lambda s: s.dma_start(out=tn.d_qk, in_=tn.qk_tok[:]), q="DS",
            samegroup=True)
        p.d("sync", lambda s: s.dma_start(
            out=tn.d_vaug, in_=tn.vaug[:, NCH * 130:2 * NCH * 130]), q="DS",
            samegroup=True, key="dbgsp")


def rec_p34(p, tn):
    bw = p.barrier_waits()
    for e in ENGS:
        p.e(e, lambda _: None, waits=bw)

    p.d("gpsimd", lambda g: g.dma_start(
        out=tn.wout_sb[:].rearrange("p (h c) -> p h c", h=8),
        in_=tn.wout.rearrange("(h pp) c -> pp h c", pp=128)), q="DG",
        key="woutld")

    allvq = [("vqsp", i) for i in range(NVQ)]
    for hc in range(8):
        p.d("sync", lambda s, hc=hc: s.dma_start_transpose(
            tn.wgT[:, hc * FI:(hc + 1) * FI],
            tn.wgq_scr[:, hc * 128:(hc + 1) * 128]), q="DS",
            waits=allvq if hc == 0 else [], samegroup=(hc > 0),
            key="wgT" if hc == 7 else None)
    for hc in range(8):
        p.d("sync", lambda s, hc=hc: s.dma_start_transpose(
            tn.wuT[:, hc * FI:(hc + 1) * FI],
            tn.wuq_scr[:, hc * 128:(hc + 1) * 128]), q="DS",
            waits=allvq if hc == 0 else [], samegroup=(hc > 0),
            key="wguT" if hc == 7 else None)
    for ncc in range(22):
        p.d("sync", lambda s, ncc=ncc: s.dma_start_transpose(
            tn.wdT[:, ncc * H:(ncc + 1) * H],
            tn.wdq_scr[:, ncc * 128:(ncc + 1) * 128]), q="DS",
            samegroup=(ncc % 8 != 0),
            key="wdT" if ncc == 21 else None)

    for t in range(NTT):
        tb = t % 2
        ldw = []
        if t == 0:
            ldw += [("attnsp", h) for h in range(NHP)]
        if t >= 1:
            ldw.append(("ostt", t - 1))
        p.d("sync", lambda s, t=t: s.dma_start(
            out=tn.aT_t[:].rearrange("p (h q) -> p h q", h=8),
            in_=tn.attnT_scr.rearrange("(h pp) q -> pp h q", pp=128)[
                :, :, t * 128:(t + 1) * 128]), q="DS", waits=ldw)
        p.d("sync", lambda s, t=t: s.dma_start(
            out=tn.xres_t[:], in_=tn.x_own[t * 128:(t + 1) * 128, :]), q="DS",
            key=("p3in", t), samegroup=True)

        opw = [("p3in", t), "woutld"]
        if t >= 1:
            opw.append(("ostt", t - 1))

        def opmm(te, t=t):
            last = None
            for half in range(2):
                for hp in range(NHP):
                    last = te.matmul(
                        tn.op_ps[:, half * 512:(half + 1) * 512],
                        tn.aT_t[:, hp * 128:(hp + 1) * 128],
                        tn.wout_sb[:, hp * 1024 + half * 512:
                                   hp * 1024 + half * 512 + 512],
                        start=(hp == 0), stop=(hp == NHP - 1))
            return last
        p.e("tensor", opmm, inc="T", key=("opmm", t), waits=opw)

        if DBG and t == 0:
            p.d("sync", lambda s: s.dma_start(out=tn.d_aT0, in_=tn.aT_t[:]),
                q="DS", waits=[("opmm", 0)])
            p.d("sync", lambda s: s.dma_start(out=tn.d_xres0, in_=tn.xres_t[:]),
                q="DS", samegroup=True, key="dbg_t0")
            p.d("sync", lambda s: s.dma_start(out=tn.d_ops0, in_=tn.y1_t[:]),
                q="DS", waits=[("lnastt", 0)], key="dbg_t0b")

        if DBG and t == 1:
            p.d("sync", lambda s: s.dma_start(
                out=tn.d_ln0[:, 0:4], in_=tn.lnsum[:]), q="DS",
                waits=[("lnadone", 0)])
            p.d("sync", lambda s: s.dma_start(
                out=tn.d_ln0[:, 4:8], in_=tn.lnsq[:]), q="DS", samegroup=True)
            p.d("sync", lambda s: s.dma_start(
                out=tn.d_ln0[:, 8:10], in_=tn.lnr[:]), q="DS", samegroup=True)
            p.d("sync", lambda s: s.dma_start(
                out=tn.d_y1t0, in_=tn.y1_t[:]), q="DS", samegroup=True,
                key="dbg_ln0")

        yvf = (lambda: tn.y1_t[:])
        ow = [("opmm", t)]
        if DBG and t == 0:
            ow.append("dbg_t0")
        if DBG and t == 1:
            ow.append("dbg_ln0")
        if t >= 2:
            ow.append(("p3sp", t - 2))
        if t >= 1:
            ow.append(("p3sp", t - 1))
        p.e("vector", lambda v, yvf=yvf: v.scalar_tensor_tensor(
            out=yvf(), in0=tn.op_ps[:], scalar=1.0, in1=tn.xres_t[:],
            op0=ALU.mult, op1=ALU.add, accum_out=tn.lnsum[:, 0:1]),
            inc="V", key=("ostt", t), waits=ow)
        p.alias(("lnastt", t), ("ostt", t))
        ln_emit(p, tn, "lna", t, yvf)

        def xquant(v, yvf=yvf):
            quant_prep(v, yvf(), H, tn.xbm, tn.xbm2, tn.qs1)
            quant_stairs(v, yvf(), tn.xq_t[:], H, tn.xbm2[:, 0:64],
                         tn.xbm[:, 0:64], tn.qs1, tn.qs3,
                         tn.qs2[:, 0:H])
            return v.tensor_scalar(out=tn.xbm2[:, 0:16], in0=tn.xbm[:, 0:16],
                                   scalar1=1.0, scalar2=None,
                                   op0=ALU.mult)
        p.e("vector", xquant, inc="V", key=("xq", t), waits=[("lnadone", t)])

        xtw = [("xq", t)]
        if t >= 1:
            xtw.append(("xqtev", t - 1))

        def xqt(te):
            last = None
            for hc in range(8):
                last = te.transpose(tn.xqT_ps[:, hc * 128:(hc + 1) * 128],
                                    tn.xq_t[:, hc * 128:(hc + 1) * 128],
                                    tn.ident[:])
            return last
        p.e("tensor", xqt, inc="T", key=("xqT", t), waits=xtw)
        evw = [("xqT", t)]
        if t >= 1:
            evw.append(("xqspill", t - 1))
        def xqtevS(s):
            s.activation(out=tn.xqT_t[:], in_=tn.xqT_ps[:], func=AF.Copy)
            return s.activation(out=tn.sqjunk[:, 0:16],
                                in_=tn.xqT_t[:, 0:16], func=AF.Copy)
        p.e("scalar", xqtevS, inc="SC", key=("xqtev", t), waits=evw)
        p.d("sync", lambda s, t=t: s.dma_start(
            out=tn.xqT_scr.rearrange("(h pp) q -> pp h q", pp=128)[
                :, :, t * 128:(t + 1) * 128],
            in_=tn.xqT_t[:].rearrange("p (h q) -> p h q", h=8)), q="DS",
            key=("xqspill", t), waits=[("xqtev", t)])
        p.d("sync", lambda s, t=t: s.dma_start(
            out=tn.y1_scr[t * 128:(t + 1) * 128, :], in_=tn.y1_t[:]), q="DS",
            key=("p3sp", t), waits=[("xq", t)])

    if DBG:
        p.d("sync", lambda s: s.dma_start(out=tn.d_y1, in_=tn.y1_scr), q="DS",
            waits=[("p3sp", t2) for t2 in range(NTT)])
        p.d("sync", lambda s: s.dma_start(out=tn.d_wgq, in_=tn.wgq_scr), q="DS",
            samegroup=True)
        p.d("sync", lambda s: s.dma_start(out=tn.d_wuq, in_=tn.wuq_scr), q="DS",
            samegroup=True)
        p.d("sync", lambda s: s.dma_start(out=tn.d_wdq, in_=tn.wdq_scr), q="DS",
            samegroup=True)
        p.d("sync", lambda s: s.dma_start(out=tn.d_xqT, in_=tn.xqT_scr), q="DS",
            samegroup=True, key="dbgsp2")

    def emit_gumm(t, ns):
        tb = t % 2
        w0 = ns * 512
        w1 = min(FI, w0 + 512)
        wd_ = w1 - w0
        guw = [("p4x", t)]
        if ns == 0:
            guw.append("wgT")
            guw.append("wguT")
        prev = ("guev", t, ns - 2) if ns >= 2 else \
            (("guev", t - 1, NS - 2 + ns) if t >= 1 else None)
        if prev is not None:
            guw.append(prev)
        if t == 0 and ns == 1:
            guw.append(("ostt", NTT - 1))

        def gumm(te, t=t, tb=tb, ns=ns, w0=w0, w1=w1, wd_=wd_):
            gps = tn.g_ps if ns % 2 == 0 else tn.op_ps_g
            ups = tn.u_ps if ns % 2 == 0 else tn.op_ps_u
            for hc in range(8):
                te.matmul(gps[:, 0:wd_],
                          tn.xqT4_t[:, tb * 1024 + hc * 128:
                                    tb * 1024 + (hc + 1) * 128],
                          tn.wgT[:, hc * FI + w0: hc * FI + w1],
                          start=(hc == 0), stop=(hc == 7))
            last = None
            for hc in range(8):
                last = te.matmul(ups[:, 0:wd_],
                                 tn.xqT4_t[:, tb * 1024 + hc * 128:
                                           tb * 1024 + (hc + 1) * 128],
                                 tn.wuT[:, hc * FI + w0: hc * FI + w1],
                                 start=(hc == 0), stop=(hc == 7))
            return last
        p.e("tensor", gumm, inc="T", key=("gumm", t, ns), waits=guw)

    def emit_hq_half(t, half):
        o0 = half * (FI // 2)
        hw = FI // 2
        nb0 = o0 // 16

        def hquant(v, o0=o0, hw=hw, nb0=nb0):
            v.tensor_tensor(out=tn.hg_t[:, o0:o0 + hw],
                            in0=tn.hg_t[:, o0:o0 + hw],
                            in1=tn.hu_t[:, o0:o0 + hw], op=ALU.mult)
            quant_prep(v, tn.hg_t[:, o0:o0 + hw], hw,
                       tn.hbm[:, nb0:nb0 + hw // 16],
                       tn.hbm2[:, nb0:nb0 + hw // 16], tn.hq_u2)
            xin = tn.hg_t[:, o0:o0 + hw]
            quant_stairs(v, xin, tn.hq_t[:, o0:o0 + hw], hw,
                         tn.hbm2[:, nb0:nb0 + hw // 16],
                         tn.hbm[:, nb0:nb0 + hw // 16],
                         tn.hu_t, tn.hq_u2, xin)
            return v.tensor_scalar(out=tn.hbm2[:, 0:8], in0=tn.hbm[:, 0:8],
                                   scalar1=1.0, scalar2=None,
                                   op0=ALU.mult)
        hqw = [("silu", t, 3)] if half == 0 else [("guev", t, NS - 1)]
        p.e("vector", hquant, inc="V", key=("hq", t, half), waits=hqw)

    def emit_p4x(t):
        if t >= NTT:
            return
        tb = t % 2
        lx = [("xqspill", t)]
        if t >= 2:
            lx.append(("gumm", t - 2, NS - 1))
        p.d("sync", lambda s, t=t, tb=tb: s.dma_start(
            out=tn.xqT4_t[:, tb * 1024:(tb + 1) * 1024].rearrange(
                "p (h q) -> p h q", h=8),
            in_=tn.xqT_scr.rearrange("(h pp) q -> pp h q", pp=128)[
                :, :, t * 128:(t + 1) * 128]), q="DS", waits=lx,
            key=("p4x", t))

    emit_p4x(0)
    emit_p4x(1)
    for t in range(NTT):
        tb = t % 2
        emit_p4x(t + 2)
        ly = [("p3sp", t)]
        if t >= 1:
            ly.append(("dstt", t - 1))
        p.d("sync", lambda s, t=t: s.dma_start(
            out=tn.y14_t[:], in_=tn.y1_scr[t * 128:(t + 1) * 128, :]), q="DS",
            key=("p4y", t), waits=ly)

        if t == 0:
            emit_gumm(0, 0)
            emit_gumm(0, 1)
        for ns in range(2, NS):
            emit_gumm(t, ns)
        if t + 1 < NTT:
            emit_gumm(t + 1, 0)
            emit_gumm(t + 1, 1)

        for ns in range(NS):
            w0 = ns * 512
            w1 = min(FI, w0 + 512)
            wd_ = w1 - w0
            sw = [("gumm", t, ns)]
            if t == 0 and ns == 0:
                sw.append(("xq", NTT - 1))
            if t >= 1 and ns == 0:
                sw.append(("hq", t - 1, 1))
            p.e("scalar", lambda s, ns=ns, w0=w0, w1=w1, wd_=wd_: s.activation(
                out=tn.hg_t[:, w0:w1],
                in_=(tn.g_ps if ns % 2 == 0 else tn.op_ps_g)[:, 0:wd_],
                func=AF.Silu), inc="SC", key=("silu", t, ns),
                waits=sw)
            def guevS(s, ns=ns, w0=w0, w1=w1, wd_=wd_, last=(ns == NS - 1)):
                r = s.activation(out=tn.hu_t[:, w0:w1],
                                 in_=(tn.u_ps if ns % 2 == 0
                                      else tn.op_ps_u)[:, 0:wd_], func=AF.Copy)
                if not last:
                    return r
                return s.activation(out=tn.sqjunk[:, 0:16],
                                    in_=tn.hu_t[:, 0:16], func=AF.Copy)
            p.e("scalar", guevS, inc="SC", key=("guev", t, ns),
                waits=[("gumm", t, ns)])
            if ns == 2:
                emit_hq_half(t, 0)
        emit_hq_half(t, 1)

        for grp in range(3):
            c0, c1 = grp * 8, min(22, grp * 8 + 8)
            if grp == 0:
                hw_ = [("hq", t, 0)]
                if t >= 1:
                    hw_.append(("dmm", t - 1))
            elif grp == 1:
                hw_ = [("hqtev", t, 0), ("hq", t, 1)]
            else:
                hw_ = [("hqtev", t, 1)]

            def hqt(te, c0=c0, c1=c1):
                last = None
                for ci in range(c0, c1):
                    last = te.transpose(
                        tn.hqT_ps[:, (ci - c0) * 128:(ci - c0 + 1) * 128],
                        tn.hq_t[:, ci * 128:(ci + 1) * 128], tn.ident[:])
                return last
            p.e("tensor", hqt, inc="T", key=("hqT", t, grp), waits=hw_)

            def hqtevS(s, c0=c0, c1=c1, last=(grp == 2)):
                r = s.activation(out=tn.hqT_t[:, c0 * 128:c1 * 128],
                                 in_=tn.hqT_ps[:, 0:(c1 - c0) * 128],
                                 func=AF.Copy)
                if not last:
                    return r
                return s.activation(out=tn.sqjunk[:, 0:16],
                                    in_=tn.hqT_t[:, 0:16], func=AF.Copy)
            p.e("scalar", hqtevS, inc="SC", key=("hqtev", t, grp),
                waits=[("hqT", t, grp)])

        dw = ["wdT", ("hqtev", t, 2)]
        if t >= 1:
            dw.append(("dstt", t - 1))

        def dmm(te):
            last = None
            for half in range(2):
                for ci in range(22):
                    last = te.matmul(
                        tn.d_ps[:, half * 512:(half + 1) * 512],
                        tn.hqT_t[:, ci * 128:(ci + 1) * 128],
                        tn.wdT[:, ci * H + half * 512: ci * H + half * 512 + 512],
                        start=(ci == 0), stop=(ci == 21))
            return last
        p.e("tensor", dmm, inc="T", key=("dmm", t), waits=dw)

        y2f = (lambda: tn.y1_t[:])
        dsw = [("dmm", t), ("p4y", t)]
        if t >= 1:
            dsw.append(("yst", t - 1))
        p.e("vector", lambda v, y2f=y2f: v.scalar_tensor_tensor(
            out=y2f(), in0=tn.d_ps[:], scalar=1.0, in1=tn.y14_t[:],
            op0=ALU.mult, op1=ALU.add, accum_out=tn.lnsum[:, 0:1]),
            inc="V", key=("dstt", t), waits=dsw)
        p.alias(("lnbstt", t), ("dstt", t))
        ln_emit(p, tn, "lnb", t, y2f)
        p.d("sync", lambda s, t=t, y2f=y2f: s.dma_start(
            out=tn.y_out[t * 128:(t + 1) * 128, :], in_=y2f()), q="DS",
            key=("yst", t), waits=[("lnbdone", t)])

    fin = [("yst", t) for t in range(NTT)]
    if DBG:
        fin += ["dbgsp", "dbgsp2"]
    p.e("sync", lambda s: None, waits=fin)


def ln_emit(p, tn, tag, t, yvf):
    p.e("vector", lambda v, yvf=yvf: v.scalar_tensor_tensor(
        out=tn.sqjunk[:], in0=yvf(), scalar=1.0, in1=yvf(),
        op0=ALU.mult, op1=ALU.mult, accum_out=tn.lnsq[:, 0:1]),
        inc="V", key=(tag + "sq", t), waits=[(tag + "stt", t)])

    def varf(v):
        v.tensor_scalar(out=tn.lnsum[:, 1:2], in0=tn.lnsum[:, 0:1],
                        scalar1=1.0 / H, scalar2=None, op0=ALU.mult)
        v.tensor_scalar(out=tn.lnsq[:, 1:2], in0=tn.lnsq[:, 0:1],
                        scalar1=1.0 / H, scalar2=None, op0=ALU.mult)
        v.tensor_tensor(out=tn.lnsum[:, 2:3], in0=tn.lnsum[:, 1:2],
                        in1=tn.lnsum[:, 1:2], op=ALU.mult)
        v.tensor_scalar(out=tn.lnsum[:, 3:4], in0=tn.lnsum[:, 1:2],
                        scalar1=1.0, scalar2=None, op0=ALU.mult)
        return v.tensor_tensor(out=tn.lnsq[:, 2:3], in0=tn.lnsq[:, 1:2],
                               in1=tn.lnsum[:, 2:3], op=ALU.subtract)
    vw = [(tag + "sq", t)]
    if DBG and tag == "lna" and t == 0:
        vw.append("dbg_t0b")
    p.e("vector", varf, inc="V", key=(tag + "var", t), waits=vw)

    def rstdf(s):
        s.activation(out=tn.lnsq[:, 3:4], in_=tn.lnsq[:, 2:3], func=AF.Ln,
                     bias=np.float32(LNEPS))
        return s.activation(out=tn.lnr[:, 0:1], in_=tn.lnsq[:, 3:4],
                            func=AF.Exp, scale=np.float32(-0.5))
    p.e("scalar", rstdf, inc="SC", key=(tag + "rstd", t), waits=[(tag + "var", t)])

    def normf(v, yvf=yvf):
        v.scalar_tensor_tensor(out=tn.lnr[:, 1:2], in0=tn.lnsum[:, 1:2],
                               scalar=-1.0, in1=tn.lnr[:, 0:1],
                               op0=ALU.mult, op1=ALU.mult)
        v.tensor_scalar(out=tn.lnsum[:, 3:4], in0=tn.lnsum[:, 1:2],
                        scalar1=1.0, scalar2=None, op0=ALU.mult)
        v.tensor_scalar(out=yvf(), in0=yvf(), scalar1=tn.lnr[:, 0:1],
                        scalar2=tn.lnr[:, 1:2], op0=ALU.mult, op1=ALU.add)
        return v.tensor_scalar(out=tn.lnsum[:, 3:4], in0=tn.lnsum[:, 1:2],
                               scalar1=1.0, scalar2=None,
                               op0=ALU.mult)
    p.e("vector", normf, inc="V", key=(tag + "done", t),
        waits=[(tag + "rstd", t)])



class TN:
    pass


def build_nc():
    nc = bass.Bass("TRN2", target_bir_lowering=False)
    tn = TN()
    D = tn.__dict__
    D["x_all"] = nc.dram_tensor("x_all", [S, H], F32, kind="ExternalInput").ap()
    D["x_own"] = nc.dram_tensor("x_own", [OWN, H], F32, kind="ExternalInput").ap()
    D["wqkv"] = nc.dram_tensor("wqkv", [H, 3 * H], F32, kind="ExternalInput").ap()
    D["wout"] = nc.dram_tensor("wout", [H, H], F32, kind="ExternalInput").ap()
    D["wg_in"] = nc.dram_tensor("wg_in", [FI, H], F32, kind="ExternalInput").ap()
    D["wu_in"] = nc.dram_tensor("wu_in", [FI, H], F32, kind="ExternalInput").ap()
    D["wd_in"] = nc.dram_tensor("wd_in", [H, FI], F32, kind="ExternalInput").ap()
    D["sintab"] = nc.dram_tensor("sintab", [128, 1024], BF16, kind="ExternalInput").ap()
    D["costab"] = nc.dram_tensor("costab", [128, 1024], BF16, kind="ExternalInput").ap()
    D["ident_in"] = nc.dram_tensor("ident_in", [128, 128], BF16, kind="ExternalInput").ap()
    D["masks_in"] = nc.dram_tensor("masks_in", [NQT * 8 * 128, QT], BF16,
                                   kind="ExternalInput").ap()
    D["cfgq"] = nc.dram_tensor("cfgq", [1, 8], I32, kind="ExternalInput").ap()
    D["y_out"] = nc.dram_tensor("y_out", [OWN, H], F32, kind="ExternalOutput").ap()
    for nm, shp, dt in [
        ("xbf_scr", [S, H], BF16), ("q_scr", [2, S, 128], BF16),
        ("k_scr", [2, S, 128], BF16), ("qown_scr", [2, OWN, 128], BF16),
        ("kperm_scr", [2, S, 128], BF16), ("attnT_scr", [NHP * 128, OWN], BF16),
        ("wqkv_bf", [H, 3 * H], BF16),
        ("wgq_scr", [FI, H], BF16), ("wuq_scr", [FI, H], BF16),
        ("wdq_scr", [H, FI], BF16), ("xqT_scr", [H, OWN], BF16),
        ("y1_scr", [OWN, H], F32),
    ]:
        D[nm] = nc.dram_tensor(nm, shp, dt, kind="Internal").ap()
    if DBG:
        D["d_qk"] = nc.dram_tensor("d_qk", [128, NCH * 256], BF16,
                                   kind="ExternalOutput").ap()
        D["d_vaug"] = nc.dram_tensor("d_vaug", [128, NCH * 130], BF16,
                                     kind="ExternalOutput").ap()
        D["d_attnT"] = nc.dram_tensor("d_attnT", [NHP * 128, OWN], BF16,
                                      kind="ExternalOutput").ap()
        D["d_y1"] = nc.dram_tensor("d_y1", [OWN, H], F32, kind="ExternalOutput").ap()
        D["d_wgq"] = nc.dram_tensor("d_wgq", [FI, H], BF16, kind="ExternalOutput").ap()
        D["d_wuq"] = nc.dram_tensor("d_wuq", [FI, H], BF16, kind="ExternalOutput").ap()
        D["d_wdq"] = nc.dram_tensor("d_wdq", [H, FI], BF16, kind="ExternalOutput").ap()
        D["d_xqT"] = nc.dram_tensor("d_xqT", [H, OWN], BF16, kind="ExternalOutput").ap()
        D["d_aT0"] = nc.dram_tensor("d_aT0", [128, 1024], BF16,
                                    kind="ExternalOutput").ap()
        D["d_ln0"] = nc.dram_tensor("d_ln0", [128, 10], F32,
                                    kind="ExternalOutput").ap()
        D["d_y1t0"] = nc.dram_tensor("d_y1t0", [128, H], F32,
                                     kind="ExternalOutput").ap()
        D["d_xres0"] = nc.dram_tensor("d_xres0", [128, H], F32,
                                      kind="ExternalOutput").ap()
        D["d_ops0"] = nc.dram_tensor("d_ops0", [128, H], F32,
                                     kind="ExternalOutput").ap()

    persist = []

    def sbp(name, shape, dt):
        g = nc.sbuf_tensor(name, shape, dt)
        persist.append(g)
        return g.__enter__()

    D["ident"] = sbp("identb", [128, 128], BF16)
    D["ones64"] = sbp("ones64b", [128, 64], BF16)
    D["cfg_sb"] = sbp("cfg_sb", [1, 8], I32)

    sems = {}
    for n in SEMS:
        g = nc.semaphore("sem_" + n)
        persist.append(g)
        sems[n] = g.__enter__()

    p12 = []

    def sb1(name, shape, dt):
        g = nc.sbuf_tensor(name, shape, dt)
        p12.append(g)
        return g.__enter__()

    def pb1(name, shape, dt):
        g = nc.psum_tensor(name, shape, dt)
        p12.append(g)
        return g.__enter__()

    D["sin_sb"] = sb1("sin_sb", [128, 1024], BF16)
    D["cos_sb"] = sb1("cos_sb", [128, 1024], BF16)
    D["xT"] = sb1("xT", [128, 8 * S], BF16)
    D["wqkv_sb"] = sb1("wqkv_sb", [128, 2 * 8 * 384], BF16)
    D["qk_tok"] = sb1("qk_tok", [128, NCH * 256], BF16)
    D["vaug"] = sb1("vaug", [128, 2 * NCH * 130], BF16)
    D["rope_t1"] = sb1("rope_t1", [128, 1024], BF16)
    D["rope_t2"] = sb1("rope_t2", [128, 1024], BF16)
    D["rope_t3"] = sb1("rope_t3", [128, 1024], BF16)
    D["qT_sb"] = sb1("qT_sb", [128, 2 * OWN], BF16)
    D["kT_sb"] = sb1("kT_sb", [128, 2 * S], BF16)
    D["pT_sb"] = sb1("pT_sb", [128, 4 * 1024], BF16)
    D["mask_sb"] = sb1("mask_sb", [128, 8 * QT], BF16)
    D["rs_sb"] = sb1("rs_sb", [128, QT], F32)
    D["rr_sb"] = sb1("rr_sb", [128, QT], F32)
    D["attnT_sb"] = sb1("attnT_sb", [128, OWN], BF16)
    D["vq_in"] = sb1("vq_in", [128, 4 * VQ_W], F32)
    D["vq_out"] = sb1("vq_out", [128, 2 * VQ_W], BF16)
    D["vq_t"] = sb1("vq_t", [128, VQ_W // 2], F32)
    D["vq_u1"] = sb1("vq_u1", [128, VQ_W // 2], F32)
    D["vq_u2"] = sb1("vq_u2", [128, VQ_W // 2], F32)
    D["vq_bm"] = sb1("vq_bm", [128, VQ_W // 16], F32)
    D["vq_bm2"] = sb1("vq_bm2", [128, VQ_W // 16], F32)
    D["qkv_ps"] = pb1("qkv_ps", [128, 1024], F32)
    D["sco_a"] = pb1("sco_a", [128, 1024], F32)
    D["sco_b"] = pb1("sco_b", [128, 1024], F32)
    D["pv_a"] = pb1("pv_a", [128, 512], F32)
    D["pv_b"] = pb1("pv_b", [128, 512], F32)

    p = P()
    record_p12_only = None

    record(p, tn)

    blk_g = nc.Block()
    blk = blk_g.__enter__()

    @blk.sync
    def _(sync):
        p.replay("sync", sync, sems, 1)

    @blk.vector
    def _(vector):
        p.replay("vector", vector, sems, 1)

    @blk.scalar
    def _(scalar):
        p.replay("scalar", scalar, sems, 1)

    @blk.tensor
    def _(tensor):
        p.replay("tensor", tensor, sems, 1)

    @blk.gpsimd
    def _(gpsimd):
        p.replay("gpsimd", gpsimd, sems, 1)

    blk_g.__exit__(None, None, None)

    for g in reversed(p12):
        g.__exit__(None, None, None)

    p34 = []

    def sb2(name, shape, dt):
        g = nc.sbuf_tensor(name, shape, dt)
        p34.append(g)
        return g.__enter__()

    def pb2(name, shape, dt):
        g = nc.psum_tensor(name, shape, dt)
        p34.append(g)
        return g.__enter__()

    D["wout_sb"] = sb2("wout_sb", [128, 8 * 1024], BF16)
    D["lnsum"] = sb2("lnsum", [128, 4], F32)
    D["lnsq"] = sb2("lnsq", [128, 4], F32)
    D["lnr"] = sb2("lnr", [128, 2], F32)
    D["wgT"] = sb2("wgT", [128, 8 * FI], BF16)
    D["wuT"] = sb2("wuT", [128, 8 * FI], BF16)
    D["wdT"] = sb2("wdT", [128, 22 * H], BF16)
    D["aT_t"] = sb2("aT_t", [128, 1024], BF16)
    D["xres_t"] = sb2("xres_t", [128, H], F32)
    D["y1_t"] = sb2("y1_t", [128, H], F32)
    D["y14_t"] = sb2("y14_t", [128, H], F32)
    D["sqjunk"] = sb2("sqjunk", [128, H], BF16)
    D["xq_t"] = sb2("xq_t", [128, H], BF16)
    D["xbm"] = sb2("xbm", [128, 64], F32)
    D["xbm2"] = sb2("xbm2", [128, 64], F32)
    D["xqT_t"] = sb2("xqT_t", [128, 1024], BF16)
    D["xqT4_t"] = sb2("xqT4_t", [128, 2 * 1024], BF16)
    D["hg_t"] = sb2("hg_t", [128, FI], F32)
    D["hu_t"] = sb2("hu_t", [128, FI], F32)
    D["hq_t"] = sb2("hq_t", [128, FI], BF16)
    D["hbm"] = sb2("hbm", [128, 176], F32)
    D["hbm2"] = sb2("hbm2", [128, 176], F32)
    D["hqT_t"] = sb2("hqT_t", [128, 22 * 128], BF16)
    D["qs1f"] = tn.hqT_t[:].bitcast(F32)
    D["hq_u2"] = tn.wout_sb[:].bitcast(F32)
    D["qs1"] = tn.hg_t
    D["qs2"] = tn.qs1f
    D["qs3"] = tn.hu_t
    D["op_ps"] = pb2("op_ps", [128, 1024], F32)
    D["xqT_ps"] = pb2("xqT_ps", [128, 1024], BF16)
    D["g_ps"] = pb2("g_ps", [128, 512], F32)
    D["u_ps"] = pb2("u_ps", [128, 512], F32)
    D["d_ps"] = pb2("d_ps", [128, 1024], F32)
    D["hqT_ps"] = pb2("hqT_ps", [128, 1024], BF16)
    D["op_ps_g"] = tn.op_ps[:, 0:512]
    D["op_ps_u"] = tn.op_ps[:, 512:1024]

    blk2_g = nc.Block("p34")
    blk2 = blk2_g.__enter__()

    @blk2.sync
    def _(sync):
        p.replay("sync", sync, sems, 2)

    @blk2.vector
    def _(vector):
        p.replay("vector", vector, sems, 2)

    @blk2.scalar
    def _(scalar):
        p.replay("scalar", scalar, sems, 2)

    @blk2.tensor
    def _(tensor):
        p.replay("tensor", tensor, sems, 2)

    @blk2.gpsimd
    def _(gpsimd):
        p.replay("gpsimd", gpsimd, sems, 2)

    blk2_g.__exit__(None, None, None)

    for g in reversed(p34):
        g.__exit__(None, None, None)
    for g in reversed(persist):
        g.__exit__(None, None, None)
    return nc



def host_prep(inputs):
    x = np.asarray(inputs["L0_attn_hidden_states"])
    w_qkv = np.asarray(inputs["L0_attn_w_qkv"])
    w_out = np.asarray(inputs["L0_attn_w_out"])
    w_gate = np.asarray(inputs["L0_ffn_w_gate"])
    w_up = np.asarray(inputs["L0_ffn_w_up"])
    w_down = np.asarray(inputs["L0_ffn_w_down"])

    evens = list(range(0, 32, 2))
    odds = list(range(1, 32, 2))
    perm = []
    for hp in range(NHP):
        ha, hb = 2 * hp, 2 * hp + 1

        def qcols(h, dperm):
            base = 384 * (h // 2) + 64 * (h % 2)
            return [base + d for d in dperm]
        qp = (qcols(ha, evens) + qcols(hb, evens) + qcols(ha, odds)
              + qcols(hb, odds) + qcols(ha, range(32, 48)) + qcols(hb, range(32, 48))
              + qcols(ha, range(48, 64)) + qcols(hb, range(48, 64)))
        kp = [c + 128 for c in qp]
        vp = ([384 * (ha // 2) + 64 * (ha % 2) + 256 + d for d in range(64)]
              + [384 * (hb // 2) + 64 * (hb % 2) + 256 + d for d in range(64)])
        perm += qp + kp + vp
    wqkv_r = np.ascontiguousarray(w_qkv[:, perm])

    inv = 1.0 / 10000.0 ** (np.arange(0, 32, 2, dtype=np.float64) / 32.0)
    ang = np.arange(S, dtype=np.float64)[:, None] * inv[None, :]

    def tab(a):
        a3 = a.reshape(NCH, 128, 16)
        out = np.zeros((128, NCH * 32), dtype=np.float32)
        for c in range(NCH):
            out[:, c * 32:c * 32 + 16] = a3[c]
            out[:, c * 32 + 16:c * 32 + 32] = a3[c]
        return out.astype(BF)
    sintab = tab(np.sin(ang).astype(np.float32))
    costab = tab(np.cos(ang).astype(np.float32))
    ident = np.eye(128, dtype=np.float32).astype(BF)

    in_maps = []
    for c in range(8):
        b, f = c // 2, c % 2
        gq = FOLD_Q[f]
        xb = np.ascontiguousarray(x[b])
        x_own = np.ascontiguousarray(
            np.concatenate([xb[g * QT:(g + 1) * QT] for g in gq], axis=0))
        cfg = np.zeros((1, 8), dtype=np.int32)
        cfg[0, :4] = gq
        masks = np.zeros((NQT, 8, 128, QT), dtype=np.float32)
        for j in range(NQT):
            Dj = DEPTH[j]
            q0 = gq[j] * QT
            for m in range(8):
                i = Dj - 8 + m
                kg = i * 128 + np.arange(128)[:, None]
                qg = q0 + np.arange(QT)[None, :]
                masks[j, m] = (kg <= qg).astype(np.float32)
        in_maps.append({
            "x_all": xb, "x_own": x_own, "wqkv": wqkv_r,
            "wout": np.ascontiguousarray(w_out),
            "wg_in": np.ascontiguousarray(w_gate),
            "wu_in": np.ascontiguousarray(w_up),
            "wd_in": np.ascontiguousarray(w_down),
            "sintab": sintab, "costab": costab, "ident_in": ident,
            "masks_in": masks.reshape(NQT * 8 * 128, QT).astype(BF),
            "cfgq": cfg,
        })
    return in_maps


_NC_CACHE = [None]
_LAST = [None]


def kernel(**inputs):
    in_maps = host_prep(inputs)
    if _NC_CACHE[0] is None:
        _NC_CACHE[0] = build_nc()
    res = run_bass_kernel_spmd(_NC_CACHE[0], in_maps, core_ids=list(range(8)))
    _LAST[0] = res
    out = np.zeros((4, S, H), dtype=np.float32)
    for c in range(8):
        b, f = c // 2, c % 2
        for j, g in enumerate(FOLD_Q[f]):
            out[b, g * QT:(g + 1) * QT] = res.results[c]["y_out"][j * QT:(j + 1) * QT]
    return out


# version 10
# speedup vs baseline: 1.1346x; 1.0792x over previous
import os
import numpy as np
import ml_dtypes
import concourse.bass as bass
import concourse.mybir as mybir
from concourse.bass_utils import run_bass_kernel_spmd

F32 = mybir.dt.float32
BF16 = mybir.dt.bfloat16
I32 = mybir.dt.int32
AF = mybir.ActivationFunctionType
ALU = mybir.AluOpType
BF = ml_dtypes.bfloat16

S, H, NHP, FI = 4096, 1024, 8, 2816
NCH, OWN, NQT, QT = 32, 2048, 4, 512
NG = NCH // 2
DEPTH = (8, 16, 24, 32)
TOT_SUPERS = sum(d // 2 for d in DEPTH)
CEXP = 3.0
LNEPS = 1e-5
MID = (0.25, 0.75, 1.25, 1.75, 2.5, 3.5, 5.0)
NTT = 16
NS = 6
DBG = 0

VQ = ([("wg", i * 128, 0, H) for i in range(22)] +
      [("wu", i * 128, 0, H) for i in range(22)] +
      [("wd", (i // 2) * 128, (i % 2) * 1408, 1408) for i in range(16)])
NVQ = len(VQ)
VQ_W = 1408
ENGS = ("sync", "vector", "scalar", "tensor", "gpsimd")
DMA_POOL = {"DS": 12, "DG": 6, "DA": 4, "DW": 8}
SEMS = (("T", "V", "SC", "G")
        + tuple(f"{q}{i}" for q, n in DMA_POOL.items() for i in range(n)))
FOLD_Q = [[0, 3, 4, 7], [1, 2, 5, 6]]
VQ_HEAD = 4
VQ_PER_QT = (1, 1, 2, 3)


class P:
    def __init__(self):
        self.cnt = {n: 0 for n in SEMS}
        self.M = {}
        self.stream = {e: [] for e in ENGS}
        self.split = {}
        self.qrot = {}
        self.curgrp = {}
        self.qreg = None

    def e(self, eng, fn, inc=None, key=None, waits=()):
        sname = None
        n = 0
        if inc is not None:
            sname, n = (inc, 1) if isinstance(inc, str) else inc
            self.cnt[sname] += n
        if key is not None:
            assert key not in self.M, ("dup milestone", key)
            self.M[key] = (sname, self.cnt[sname])
        self.stream[eng].append((tuple(waits), fn, sname, n))

    def d(self, eng, fn, q="DS", key=None, waits=(), samegroup=False):
        if not samegroup:
            i = self.qrot[q] = (self.qrot.get(q, -1) + 1) % DMA_POOL[q]
            s = f"{q}{i}"
            self.curgrp[q] = s
            prev = self.cnt[s]
            waits = tuple(waits) + ((("_raw", s, prev),) if prev > 0 else ())
        else:
            s = self.curgrp[q]
        self.e(eng, fn, inc=(s, 16), key=key, waits=waits)

    def alias(self, newkey, oldkey):
        assert newkey not in self.M
        self.M[newkey] = self.M[oldkey]

    def mark_split(self):
        for e in ENGS:
            self.split[e] = len(self.stream[e])

    def barrier_waits(self):
        return [("_raw", s, self.cnt[s]) for s in SEMS]

    def replay(self, engname, eng, sems, phase):
        lo = 0 if phase == 1 else self.split[engname]
        hi = self.split[engname] if phase == 1 else len(self.stream[engname])
        for waits, fn, sname, n in self.stream[engname][lo:hi]:
            for w in waits:
                if isinstance(w, tuple) and len(w) == 3 and w[0] == "_raw":
                    ws, val = w[1], w[2]
                else:
                    ws, val = self.M[w]
                if val > 0:
                    eng.wait_ge(sems[ws], val)
            inst = fn(eng)
            if sname is not None:
                assert inst is not None, (engname, sname)
                inst.then_inc(sems[sname], n)


def quant_stairs(eng, xin, out, width, rinv, scl, t_t, sgn_t, u1):
    nb = width // 16
    rb = rinv.unsqueeze(2).broadcast_to([128, nb, 16])
    sb_ = scl.unsqueeze(2).broadcast_to([128, nb, 16])
    tt = t_t[:, 0:width]
    sg = sgn_t[:, 0:width]
    uu1 = u1
    eng.tensor_tensor(out=tt.rearrange("p (b k) -> p b k", k=16),
                      in0=xin.rearrange("p (b k) -> p b k", k=16),
                      in1=rb, op=ALU.mult)
    eng.tensor_scalar(out=sg, in0=tt, scalar1=0.0, scalar2=0.5,
                      op0=ALU.is_ge, op1=ALU.subtract)
    eng.scalar_tensor_tensor(out=tt, in0=tt, scalar=2.0, in1=sg,
                             op0=ALU.mult, op1=ALU.mult)
    eng.tensor_scalar(out=uu1, in0=tt, scalar1=MID[0], scalar2=None,
                      op0=ALU.is_gt)
    for m in (MID[1], MID[2], MID[3]):
        eng.scalar_tensor_tensor(out=uu1, in0=tt, scalar=float(m), in1=uu1,
                                 op0=ALU.is_gt, op1=ALU.add)
    eng.tensor_scalar(out=out, in0=tt, scalar1=MID[4], scalar2=None,
                      op0=ALU.is_gt)
    eng.scalar_tensor_tensor(out=out, in0=tt, scalar=MID[5], in1=out,
                             op0=ALU.is_gt, op1=ALU.add)
    eng.scalar_tensor_tensor(out=out, in0=tt, scalar=MID[6], in1=out,
                             op0=ALU.is_gt, op1=ALU.add)
    eng.scalar_tensor_tensor(out=uu1, in0=uu1, scalar=0.5, in1=out,
                             op0=ALU.mult, op1=ALU.add)
    eng.scalar_tensor_tensor(out=uu1, in0=tt, scalar=MID[6], in1=uu1,
                             op0=ALU.is_gt, op1=ALU.add)
    eng.tensor_tensor(out=uu1, in0=uu1, in1=sg, op=ALU.mult)
    return eng.scalar_tensor_tensor(
        out=out.rearrange("p (b k) -> p b k", k=16),
        in0=uu1.rearrange("p (b k) -> p b k", k=16),
        scalar=2.0, in1=sb_, op0=ALU.mult, op1=ALU.mult)


MID2 = tuple(float(m) * float(m) for m in MID)


def quant_prep(eng, xin, width, bm, bm2, junk):
    nb = width // 16
    eng.tensor_reduce(out=bm[:, 0:nb],
                      in_=xin.rearrange("p (b k) -> p b k", k=16),
                      axis=mybir.AxisListType.X, op=ALU.max,
                      apply_absolute_value=True)
    eng.tensor_scalar(out=junk[:, 0:nb], in0=xin[:, 0:nb], scalar1=1.0,
                      scalar2=None, op0=ALU.mult)
    eng.tensor_scalar(out=bm[:, 0:nb], in0=bm[:, 0:nb], scalar1=1.0 / 6.0,
                      scalar2=1e-12, op0=ALU.mult, op1=ALU.max)
    eng.tensor_scalar(out=junk[:, 0:nb], in0=xin[:, 0:nb], scalar1=1.0,
                      scalar2=None, op0=ALU.mult)
    eng.reciprocal(out=bm2[:, 0:nb], in_=bm[:, 0:nb])
    return eng.tensor_scalar(out=junk[:, 0:nb], in0=xin[:, 0:nb], scalar1=1.0,
                             scalar2=None, op0=ALU.mult)



def record(p, tn):
    rec_sync_p0(p, tn)
    rec_gp_p12(p, tn)
    rec_tensor_p12(p, tn)
    rec_vector_p12(p, tn)
    rec_scalar_p12(p, tn)
    rec_sync_p12(p, tn)
    p.mark_split()
    rec_p34(p, tn)


def rec_sync_p0(p, tn):
    p.d("sync", lambda s: s.dma_start(out=tn.cfg_sb[:], in_=tn.cfgq), key="cfg")
    p.d("sync", lambda s: s.dma_start(out=tn.ident[:], in_=tn.ident_in))
    p.d("sync", lambda s: s.dma_start(out=tn.sin_sb[:], in_=tn.sintab),
        samegroup=True)
    p.d("sync", lambda s: s.dma_start(out=tn.cos_sb[:], in_=tn.costab),
        samegroup=True, key="tabs")

    def ld(s):
        p.qreg = [s.value_load(tn.cfg_sb[0:1, j:j + 1]) for j in range(NQT)]
        return None
    p.e("sync", ld, waits=["cfg"])


def rec_gp_p12(p, tn):
    def vq_load(i):
        if i >= NVQ:
            return
        wname, r0, c0, width = VQ[i]
        src_ = {"wg": tn.wg_in, "wu": tn.wu_in, "wd": tn.wd_in}[wname]
        b = i % 4
        p.d("gpsimd", lambda g, src_=src_, r0=r0, c0=c0, width=width, b=b:
            g.dma_start(out=tn.vq_in[:, b * VQ_W:b * VQ_W + width],
                        in_=src_[r0:r0 + 128, c0:c0 + width]), q="DW",
            key=("vqin", i))

    vq_done = [0]

    def vq_io(nmax):
        done = 0
        while vq_done[0] < NVQ and done < nmax:
            i = vq_done[0]
            wname, r0, c0, width = VQ[i]
            dst = {"wg": tn.wgq_scr, "wu": tn.wuq_scr, "wd": tn.wdq_scr}[wname]
            sw = [("vqrs", i + 1)] if i + 1 < NVQ else ["vqflush"]
            p.d("gpsimd", lambda g, dst=dst, r0=r0, c0=c0, width=width, i=i:
                g.dma_start(out=dst[r0:r0 + 128, c0:c0 + width],
                            in_=tn.vq_out[:, (i % 2) * VQ_W:
                                          (i % 2) * VQ_W + width]), q="DW",
                key=("vqsp", i), waits=sw)
            vq_load(i + 4)
            vq_done[0] += 1
            done += 1

    vq_load(0)
    vq_load(1)
    for i in range(4):
        p.d("gpsimd", lambda g, i=i: g.dma_start(
            out=tn.xbf_scr[i * 1024:(i + 1) * 1024, :],
            in_=tn.x_all[i * 1024:(i + 1) * 1024, :]), q="DG",
            samegroup=(i > 0), key="xcast" if i == 3 else None)
    vq_load(2)
    vq_load(3)

    p.d("gpsimd", lambda g: g.dma_start(out=tn.wqkv_bf, in_=tn.wqkv),
        q="DG", key="wqkvcast")
    for hp in range(NHP):
        vq_io(7)
    vq_io(NVQ)


def rec_tensor_p12(p, tn):
    for hp in range(NHP):
        att = attention_qt_items(p, tn, hp - 1) if hp >= 1 else [[], [], [], []]
        for j4 in range(NQT):
            for c in range(8 * j4, 8 * j4 + 8):
                qkv_chunk_item(p, tn, hp, c)()
            for fn in att[j4]:
                fn()
    for blk in attention_qt_items(p, tn, NHP - 1):
        for fn in blk:
            fn()


def qkv_chunk_item(p, tn, hp, c):
    buf = hp % 2
    waits = []
    if c == 0:
        waits += ["xT", ("wqkv", hp)]
        if hp >= 1:
            waits.append(("qout", hp - 1))
            waits.append(("qkout", hp - 1))
        if hp >= 2:
            waits.append(("pvs", hp - 2, TOT_SUPERS - 1))
    elif c == 1:
        if hp >= 1:
            waits.append(("qevict", hp - 1, NCH - 1))
    else:
        waits.append(("qevict", hp, c - 2))

    def qmm(te, hp=hp, c=c, buf=buf):
        last = None
        h0 = (c % 2) * 512
        for hc in range(8):
            last = te.matmul(
                tn.qkv_ps[:, h0:h0 + 384],
                tn.xT[:, hc * S + c * 128: hc * S + (c + 1) * 128],
                tn.wqkv_sb[:, buf * 3072 + hc * 384:
                           buf * 3072 + (hc + 1) * 384],
                start=(hc == 0), stop=(hc == 7))
        return last
    return lambda: p.e("tensor", qmm, inc="T", key=("qkvmm", hp, c),
                       waits=waits)


def attention_qt_items(p, tn, hp):
    buf = hp % 2
    blocks = []
    sup = 0
    for j in range(NQT):
        items = []
        D = DEPTH[j]
        for s in range(D // 2):
            waits = []
            if sup == 0:
                waits.append(("qT", hp))
                waits.append(("qkT", hp))
            if sup > 0:
                waits.append(("exp", hp, sup - 1, 1))

            def qk(te, hp=hp, j=j, s=s, buf=buf):
                lastB = None
                for ii in range(2):
                    i = 2 * s + ii
                    for head in range(2):
                        dst = tn.sco_a if head == 0 else tn.sco_b
                        lm = te.matmul(
                            dst[:, ii * 512:(ii + 1) * 512],
                            tn.kT_sb[head * 64:(head + 1) * 64,
                                     buf * S + i * 128: buf * S + (i + 1) * 128],
                            tn.qT_sb[head * 64:(head + 1) * 64,
                                     buf * OWN + j * QT: buf * OWN + (j + 1) * QT],
                            start=True, stop=True, tile_position=(head * 64, 0))
                        if head == 1:
                            lastB = lm
                return lastB
            items.append(lambda qk=qk, hp=hp, sup=sup, waits=tuple(waits):
                         p.e("tensor", qk, inc="T", key=("qk", hp, sup),
                             waits=waits))
            if s > 0:
                items.append(pv_super_item(p, tn, hp, j, s - 1, sup - 1))
            sup += 1
        items.append(pv_super_item(p, tn, hp, j, D // 2 - 1, sup - 1))
        items.append(lambda p=p, hp=hp, j=j, sup=sup:
                     p.alias(("pvq", hp, j), ("pvs", hp, sup - 1)))
        blocks.append(items)
    return blocks


def pv_super_item(p, tn, hp, j, s, sup):
    D = DEPTH[j]
    buf = hp % 2
    waits = [("exp", hp, sup, 0), ("exp", hp, sup, 1), ("maskv", hp, sup)]
    if s == 0:
        if j > 0:
            waits.append(("normq", hp, j - 1))
        elif hp > 0:
            waits.append(("normq", hp - 1, NQT - 1))
    pbuf = sup % 2

    def pv(te, hp=hp, j=j, s=s, pbuf=pbuf, buf=buf, D=D):
        last = None
        for ii in range(2):
            i = 2 * s + ii
            for head in range(2):
                pvp = tn.pv_a if head == 0 else tn.pv_b
                pt = tn.pT_sb[:, (head * 2 + pbuf) * 1024 + ii * 512:
                              (head * 2 + pbuf) * 1024 + (ii + 1) * 512]
                te.matmul(pvp[0:64, :],
                          tn.vaug[:, buf * NCH * 130 + i * 130 + head * 65:
                                  buf * NCH * 130 + i * 130 + head * 65 + 64],
                          pt, start=(i == 0), stop=(i == D - 1),
                          tile_position=(0, 0), skip_group_check=True)
                last = te.matmul(pvp[64:128, :], tn.ones64[:], pt,
                                 start=(i == 0), stop=(i == D - 1),
                                 tile_position=(0, 64), skip_group_check=True)
        return last
    return lambda: p.e("tensor", pv, inc="T", key=("pvs", hp, sup),
                       waits=waits)


def rec_vector_p12(p, tn):
    vq_next = [0]

    def do_vq(nmax):
        done = 0
        while vq_next[0] < NVQ and done < nmax:
            i = vq_next[0]
            wname, r0, c0, width = VQ[i]
            b = i % 4

            def prep(v, i=i, width=width, b=b):
                return quant_prep(v, tn.vq_in[:, b * VQ_W:b * VQ_W + width],
                                  width, tn.vq_bm, tn.vq_bm2, tn.vq_t)
            p.e("vector", prep, inc="V", key=("vqrs", i), waits=[("vqin", i)])

            def qf(v, i=i, width=width, b=b):
                hw = width // 2
                last = None
                for off in (0, hw):
                    xin = tn.vq_in[:, b * VQ_W + off:b * VQ_W + off + hw]
                    last = quant_stairs(
                        v, xin,
                        tn.vq_out[:, (i % 2) * VQ_W + off:
                                  (i % 2) * VQ_W + off + hw],
                        hw, tn.vq_bm2[:, off // 16:(off + hw) // 16],
                        tn.vq_bm[:, off // 16:(off + hw) // 16],
                        tn.vq_t, tn.vq_u2, tn.vq_u1[:, 0:hw])
                return last
            qw = [("vqsp", i - 2)] if i >= 2 else []
            p.e("vector", qf, inc="V", key=("vqq", i), waits=qw)
            vq_next[0] += 1
            done += 1
    p.e("vector", lambda v: v.memset(tn.ones64[:], 1.0), inc="V", key="ones")
    do_vq(VQ_HEAD)
    for hp in range(NHP):
        buf = hp % 2

        def rope(v, half=0):
            qk = tn.qk_tok[:].rearrange("p (c qk r) -> p c qk r", qk=2, r=128)
            c0, c1 = half * 16, (half + 1) * 16
            Ev = qk[:, c0:c1, :, 0:32]
            Ov = qk[:, c0:c1, :, 32:64]
            cosv = tn.cos_sb[:].rearrange("p (c w) -> p c w", w=32)[
                :, c0:c1].unsqueeze(2).broadcast_to([128, 16, 2, 32])
            sinv = tn.sin_sb[:].rearrange("p (c w) -> p c w", w=32)[
                :, c0:c1].unsqueeze(2).broadcast_to([128, 16, 2, 32])
            t1 = tn.rope_t1[:].rearrange("p (c qk r) -> p c qk r", qk=2, r=32)
            t2 = tn.rope_t2[:].rearrange("p (c qk r) -> p c qk r", qk=2, r=32)
            t3 = tn.rope_t3[:].rearrange("p (c qk r) -> p c qk r", qk=2, r=32)
            v.tensor_tensor(out=t1, in0=Ev, in1=cosv, op=ALU.mult)
            v.tensor_tensor(out=t2, in0=Ov, in1=sinv, op=ALU.mult)
            v.tensor_tensor(out=t1, in0=t1, in1=t2, op=ALU.subtract)
            v.tensor_tensor(out=t2, in0=Ev, in1=sinv, op=ALU.mult)
            v.tensor_tensor(out=t3, in0=Ov, in1=cosv, op=ALU.mult)
            v.tensor_tensor(out=Ov, in0=t2, in1=t3, op=ALU.add)
            return v.tensor_copy(out=Ev, in_=t1)

        def rope_end(v):
            rope(v, 1)
            return v.tensor_scalar(out=tn.rope_t1[:, 0:16],
                                   in0=tn.rope_t2[:, 0:16], scalar1=1.0,
                                   scalar2=None, op0=ALU.mult)

        vw = [("pvs", hp - 2, TOT_SUPERS - 1)] if hp >= 2 else []
        p.e("vector", lambda v, buf=buf: v.memset(
            tn.vaug[:, buf * NCH * 130:(buf + 1) * NCH * 130], 1.0),
            inc="V", key=("vaugset", hp), waits=vw)

        for j4 in range(NQT):
            if j4 == 3:
                p.e("vector", lambda v: rope(v, 0),
                    waits=["tabs", ("qevict", hp, NCH - 1)])
                p.e("vector", rope_end, inc="V", key=("rope", hp))
            if hp >= 1:
                attv_qt(p, tn, hp - 1, j4)
            do_vq(VQ_PER_QT[j4])
        if hp >= 1:
            p.alias(("norm", hp - 1), ("normq", hp - 1, NQT - 1))
    for j4 in range(NQT):
        attv_qt(p, tn, NHP - 1, j4)
    p.alias(("norm", NHP - 1), ("normq", NHP - 1, NQT - 1))
    do_vq(NVQ)
    p.e("vector", lambda v: v.tensor_scalar(
        out=tn.rope_t1[:, 0:16], in0=tn.rope_t2[:, 0:16], scalar1=1.0,
        scalar2=None, op0=ALU.mult), inc="V", key="vqflush")


def attv_qt(p, tn, hp, j):
    buf = hp % 2
    sup = sum(DEPTH[jj] // 2 for jj in range(j))
    D = DEPTH[j]
    for s in range(D // 2):
        pbuf = sup % 2
        masked = [2 * s + ii for ii in range(2) if 2 * s + ii >= D - 8]
        if masked:
            def mk(v, s=s, pbuf=pbuf, masked=tuple(masked), D=D):
                last = None
                for i in masked:
                    m = i - (D - 8)
                    ii = i - 2 * s
                    for head in range(2):
                        pt = tn.pT_sb[:, (head * 2 + pbuf) * 1024 + ii * 512:
                                      (head * 2 + pbuf) * 1024 + (ii + 1) * 512]
                        last = v.tensor_tensor(
                            out=pt, in0=pt,
                            in1=tn.mask_sb[:, m * QT:(m + 1) * QT],
                            op=ALU.mult)
                return last
            p.e("vector", mk, inc="V", key=("maskv", hp, sup),
                waits=[("exp", hp, sup, 0), ("exp", hp, sup, 1),
                       ("mload", hp, j)])
        else:
            p.M[("maskv", hp, sup)] = ("V", p.cnt["V"])
        sup += 1

    def sums(v):
        v.tensor_copy(out=tn.rs_sb[0:64, :], in_=tn.pv_a[64:128, :])
        return v.tensor_copy(out=tn.rs_sb[64:128, :], in_=tn.pv_b[64:128, :])
    p.e("vector", sums, inc="V", key=("sums", hp, j), waits=[("pvq", hp, j)])

    nw = [("recip", hp, j)]
    if j == 0 and hp >= 1:
        nw.append(("attnsp", hp - 1))

    def norm(v, j=j):
        v.tensor_tensor(
            out=tn.attnT_sb[0:64, j * QT:(j + 1) * QT],
            in0=tn.pv_a[0:64, :], in1=tn.rr_sb[0:64, :], op=ALU.mult)
        v.tensor_tensor(
            out=tn.attnT_sb[64:128, j * QT:(j + 1) * QT],
            in0=tn.pv_b[0:64, :], in1=tn.rr_sb[64:128, :], op=ALU.mult)
        return v.tensor_scalar(out=tn.rope_t3[:, 0:16],
                               in0=tn.rope_t2[:, 0:16], scalar1=1.0,
                               scalar2=None, op0=ALU.mult)
    p.e("vector", norm, inc="V", key=("normq", hp, j), waits=nw)


def rec_scalar_p12(p, tn):
    def wqkv_ld(hp):
        if hp >= NHP:
            return
        buf = hp % 2
        w = ["wqkvcast"] if hp < 2 else [("qkvmm", hp - 2, NCH - 1)]
        p.d("scalar", lambda s, hp=hp, buf=buf: s.dma_start(
            out=tn.wqkv_sb[:, buf * 3072:(buf + 1) * 3072].rearrange(
                "p (hc c) -> p hc c", hc=8),
            in_=tn.wqkv_bf.rearrange("(hc pp) c -> pp hc c", pp=128)[
                :, :, hp * 384:(hp + 1) * 384]), q="DA",
            key=("wqkv", hp), waits=w)

    def evictS(it, c):
        buf = it % 2
        waits = [("qkvmm", it, c)]
        if c == 0:
            waits.append(("vaugset", it))

        def ev(s, it=it, c=c, buf=buf):
            h0 = (c % 2) * 512
            s.activation(
                out=tn.qk_tok[:].rearrange("p (c r) -> p c r", r=256)[
                    :, c:c + 1, :],
                in_=tn.qkv_ps[:, h0:h0 + 256], func=AF.Copy)
            last = s.activation(
                out=tn.vaug[:, buf * NCH * 130:(buf + 1) * NCH * 130].rearrange(
                    "p (c h sx) -> p c h sx", h=2, sx=65)[
                    :, c:c + 1, :, 0:64],
                in_=tn.qkv_ps[:, h0 + 256:h0 + 384].rearrange(
                    "p (h d) -> p h d", d=64), func=AF.Copy)
            if c == NCH - 1:
                last = s.activation(out=tn.scjunk[:],
                                    in_=tn.qk_tok[:, 0:16], func=AF.Copy)
            return last
        p.e("scalar", ev, inc="SC", key=("qevict", it, c), waits=waits)

    def exps_qt(hp, j):
        sup0 = sum(DEPTH[jj] // 2 for jj in range(j))
        D = DEPTH[j]
        mwaits = []
        if hp * NQT + j >= 1:
            jj = hp * NQT + j - 1
            mwaits.append(("normq", jj // NQT, jj % NQT))
        p.d("scalar", lambda s, j=j: s.dma_start(
            out=tn.mask_sb[:].rearrange("p (m q) -> p m q", m=8),
            in_=tn.masks_in.rearrange("(j m pp) q -> pp j m q",
                                      m=8, pp=128)[:, j]),
            q="DA", key=("mload", hp, j), waits=mwaits)
        sup = sup0
        for s_ in range(D // 2):
            pbuf = sup % 2
            for head in range(2):
                waits = [("qk", hp, sup)]
                if sup >= 2:
                    waits.append(("pvs", hp, sup - 2))

                def ex(sc, head=head, pbuf=pbuf):
                    src_ = tn.sco_a if head == 0 else tn.sco_b
                    return sc.activation(
                        out=tn.pT_sb[:, (head * 2 + pbuf) * 1024:
                                     (head * 2 + pbuf + 1) * 1024],
                        in_=src_[:], func=AF.Exp,
                        bias=np.float32(-CEXP), scale=np.float32(0.125))
                p.e("scalar", ex, inc="SC", key=("exp", hp, sup, head),
                    waits=waits)
            sup += 1

        def recip(sc):
            sc.activation(out=tn.rr_sb[:], in_=tn.rs_sb[:], func=AF.Ln)
            return sc.activation(out=tn.rr_sb[:], in_=tn.rr_sb[:],
                                 func=AF.Exp, scale=np.float32(-1.0))
        p.e("scalar", recip, inc="SC", key=("recip", hp, j),
            waits=[("sums", hp, j)])

    wqkv_ld(0)
    wqkv_ld(1)
    for it in range(NHP):
        for j4 in range(NQT):
            for c in range(8 * j4, 8 * j4 + 8):
                evictS(it, c)
            if it >= 1:
                exps_qt(it - 1, j4)
        wqkv_ld(it + 2)
    for j4 in range(NQT):
        exps_qt(NHP - 1, j4)


def rec_sync_p12(p, tn):
    for hc in range(8):
        p.d("sync", lambda s, hc=hc: s.dma_start_transpose(
            tn.xT[:, hc * S:(hc + 1) * S],
            tn.xbf_scr[:, hc * 128:(hc + 1) * 128]), q="DS",
            waits=["xcast", "tabs"] if hc == 0 else [], samegroup=(hc > 0),
            key="xT" if hc == 7 else None)
    for hp in range(NHP):
        buf = hp % 2
        p.d("sync", lambda s, buf=buf: s.dma_start(
            out=tn.q_scr[buf].rearrange("(c pp) d -> pp c d", pp=128),
            in_=tn.qk_tok[:].rearrange("p (c r) -> p c r", r=256)[:, :, 0:128]),
            q="DS", waits=[("rope", hp)], key=("qout", hp))
        p.d("sync", lambda s, buf=buf: s.dma_start(
            out=tn.k_scr[buf].rearrange("(c pp) d -> pp c d", pp=128),
            in_=tn.qk_tok[:].rearrange("p (c r) -> p c r", r=256)[:, :, 128:256]),
            q="DS", key=("qkout", hp))
        first = True
        for j in range(NQT):
            for hh in range(2):
                p.d("sync", lambda s, j=j, buf=buf, hh=hh: s.dma_start(
                    out=tn.qown_scr[buf][j * QT:(j + 1) * QT,
                                         hh * 64:(hh + 1) * 64],
                    in_=tn.q_scr[buf][bass.ts(p.qreg[j], QT), :].rearrange(
                        "r (g h k) -> r h g k", g=4, h=2)[:, hh]), q="DS",
                    waits=[("qout", hp)] if first else [],
                    samegroup=not first,
                    key=("qgather", hp) if (j == NQT - 1 and hh == 1) else None)
                first = False
        for hh in range(2):
            p.d("sync", lambda s, buf=buf, hh=hh: s.dma_start(
                out=tn.kperm_scr[buf][:, hh * 64:(hh + 1) * 64],
                in_=tn.k_scr[buf].rearrange(
                    "r (g h k) -> r h g k", g=4, h=2)[:, hh]), q="DS",
                waits=[("qkout", hp)] if hh == 0 else [], samegroup=(hh == 1),
                key=("qkperm", hp) if hh == 1 else None)
        twaits = [("qgather", hp)]
        if hp >= 2:
            twaits.append(("pvs", hp - 2, TOT_SUPERS - 1))
        p.d("sync", lambda s, buf=buf: s.dma_start_transpose(
            tn.qT_sb[:, buf * OWN:(buf + 1) * OWN], tn.qown_scr[buf]),
            q="DS", waits=twaits, key=("qT", hp))
        p.d("sync", lambda s, buf=buf: s.dma_start_transpose(
            tn.kT_sb[:, buf * S:(buf + 1) * S], tn.kperm_scr[buf]),
            q="DS", waits=[("qkperm", hp)], key=("qkT", hp))
        p.d("sync", lambda s, hp=hp: s.dma_start(
            out=tn.attnT_scr[hp * 128:(hp + 1) * 128, :],
            in_=tn.attnT_sb[:]), q="DS", key=("attnsp", hp),
            waits=[("norm", hp)])
    if DBG:
        p.d("sync", lambda s: s.dma_start(out=tn.d_attnT, in_=tn.attnT_scr),
            q="DS", waits=[("attnsp", NHP - 1)])
        p.d("sync", lambda s: s.dma_start(out=tn.d_qk, in_=tn.qk_tok[:]), q="DS",
            samegroup=True)
        p.d("sync", lambda s: s.dma_start(
            out=tn.d_vaug, in_=tn.vaug[:, NCH * 130:2 * NCH * 130]), q="DS",
            samegroup=True, key="dbgsp")


def rec_p34(p, tn):
    bw = p.barrier_waits()
    for e in ENGS:
        p.e(e, lambda _: None, waits=bw)

    p.d("gpsimd", lambda g: g.dma_start(
        out=tn.wout_sb[:].rearrange("p (h c) -> p h c", h=8),
        in_=tn.wout.rearrange("(h pp) c -> pp h c", pp=128)), q="DG",
        key="woutld")

    allvq = [("vqsp", i) for i in range(NVQ)]
    for hc in range(8):
        p.d("sync", lambda s, hc=hc: s.dma_start_transpose(
            tn.wgT[:, hc * FI:(hc + 1) * FI],
            tn.wgq_scr[:, hc * 128:(hc + 1) * 128]), q="DS",
            waits=allvq if hc == 0 else [], samegroup=(hc > 0),
            key="wgT" if hc == 7 else None)
    for hc in range(8):
        p.d("sync", lambda s, hc=hc: s.dma_start_transpose(
            tn.wuT[:, hc * FI:(hc + 1) * FI],
            tn.wuq_scr[:, hc * 128:(hc + 1) * 128]), q="DS",
            waits=allvq if hc == 0 else [], samegroup=(hc > 0),
            key="wguT" if hc == 7 else None)
    for ncc in range(22):
        p.d("sync", lambda s, ncc=ncc: s.dma_start_transpose(
            tn.wdT[:, ncc * H:(ncc + 1) * H],
            tn.wdq_scr[:, ncc * 128:(ncc + 1) * 128]), q="DS",
            samegroup=(ncc % 8 != 0),
            key="wdT" if ncc == 21 else None)

    for t in range(NTT):
        tb = t % 2
        ldw = []
        if t == 0:
            ldw += [("attnsp", h) for h in range(NHP)]
        if t >= 1:
            ldw.append(("ostt", t - 1))
        p.d("sync", lambda s, t=t: s.dma_start(
            out=tn.aT_t[:].rearrange("p (h q) -> p h q", h=8),
            in_=tn.attnT_scr.rearrange("(h pp) q -> pp h q", pp=128)[
                :, :, t * 128:(t + 1) * 128]), q="DS", waits=ldw)
        p.d("sync", lambda s, t=t: s.dma_start(
            out=tn.xres_t[:], in_=tn.x_own[t * 128:(t + 1) * 128, :]), q="DS",
            key=("p3in", t), samegroup=True)

        opw = [("p3in", t), "woutld"]
        if t >= 1:
            opw.append(("ostt", t - 1))

        def opmm(te, t=t):
            last = None
            for half in range(2):
                for hp in range(NHP):
                    last = te.matmul(
                        tn.op_ps[:, half * 512:(half + 1) * 512],
                        tn.aT_t[:, hp * 128:(hp + 1) * 128],
                        tn.wout_sb[:, hp * 1024 + half * 512:
                                   hp * 1024 + half * 512 + 512],
                        start=(hp == 0), stop=(hp == NHP - 1))
            return last
        p.e("tensor", opmm, inc="T", key=("opmm", t), waits=opw)

        if DBG and t == 0:
            p.d("sync", lambda s: s.dma_start(out=tn.d_aT0, in_=tn.aT_t[:]),
                q="DS", waits=[("opmm", 0)])
            p.d("sync", lambda s: s.dma_start(out=tn.d_xres0, in_=tn.xres_t[:]),
                q="DS", samegroup=True, key="dbg_t0")
            p.d("sync", lambda s: s.dma_start(out=tn.d_ops0, in_=tn.y1_t[:]),
                q="DS", waits=[("lnastt", 0)], key="dbg_t0b")

        if DBG and t == 1:
            p.d("sync", lambda s: s.dma_start(
                out=tn.d_ln0[:, 0:4], in_=tn.lnsum[:]), q="DS",
                waits=[("lnadone", 0)])
            p.d("sync", lambda s: s.dma_start(
                out=tn.d_ln0[:, 4:8], in_=tn.lnsq[:]), q="DS", samegroup=True)
            p.d("sync", lambda s: s.dma_start(
                out=tn.d_ln0[:, 8:10], in_=tn.lnr[:]), q="DS", samegroup=True)
            p.d("sync", lambda s: s.dma_start(
                out=tn.d_y1t0, in_=tn.y1_t[:]), q="DS", samegroup=True,
                key="dbg_ln0")

        yvf = (lambda: tn.y1_t[:])
        ow = [("opmm", t)]
        if DBG and t == 0:
            ow.append("dbg_t0")
        if DBG and t == 1:
            ow.append("dbg_ln0")
        if t >= 2:
            ow.append(("p3sp", t - 2))
        if t >= 1:
            ow.append(("p3sp", t - 1))
        p.e("vector", lambda v, yvf=yvf: v.scalar_tensor_tensor(
            out=yvf(), in0=tn.op_ps[:], scalar=1.0, in1=tn.xres_t[:],
            op0=ALU.mult, op1=ALU.add, accum_out=tn.lnsum[:, 0:1]),
            inc="V", key=("ostt", t), waits=ow)
        p.alias(("lnastt", t), ("ostt", t))
        ln_emit(p, tn, "lna", t, yvf)

        def xquant(v, yvf=yvf):
            quant_prep(v, yvf(), H, tn.xbm, tn.xbm2, tn.qs1)
            quant_stairs(v, yvf(), tn.xq_t[:], H, tn.xbm2[:, 0:64],
                         tn.xbm[:, 0:64], tn.qs1, tn.qs3,
                         tn.qs2[:, 0:H])
            return v.tensor_scalar(out=tn.xbm2[:, 0:16], in0=tn.xbm[:, 0:16],
                                   scalar1=1.0, scalar2=None,
                                   op0=ALU.mult)
        p.e("vector", xquant, inc="V", key=("xq", t), waits=[("lnadone", t)])

        xtw = [("xq", t)]
        if t >= 1:
            xtw.append(("xqtev", t - 1))

        def xqt(te):
            last = None
            for hc in range(8):
                last = te.transpose(tn.xqT_ps[:, hc * 128:(hc + 1) * 128],
                                    tn.xq_t[:, hc * 128:(hc + 1) * 128],
                                    tn.ident[:])
            return last
        p.e("tensor", xqt, inc="T", key=("xqT", t), waits=xtw)
        evw = [("xqT", t)]
        if t >= 1:
            evw.append(("xqspill", t - 1))
        def xqtevS(s):
            s.activation(out=tn.xqT_t[:], in_=tn.xqT_ps[:], func=AF.Copy)
            return s.activation(out=tn.sqjunk[:, 0:16],
                                in_=tn.xqT_t[:, 0:16], func=AF.Copy)
        p.e("scalar", xqtevS, inc="SC", key=("xqtev", t), waits=evw)
        p.d("sync", lambda s, t=t: s.dma_start(
            out=tn.xqT_scr.rearrange("(h pp) q -> pp h q", pp=128)[
                :, :, t * 128:(t + 1) * 128],
            in_=tn.xqT_t[:].rearrange("p (h q) -> p h q", h=8)), q="DS",
            key=("xqspill", t), waits=[("xqtev", t)])
        p.d("sync", lambda s, t=t: s.dma_start(
            out=tn.y1_scr[t * 128:(t + 1) * 128, :], in_=tn.y1_t[:]), q="DS",
            key=("p3sp", t), waits=[("xq", t)])

    if DBG:
        p.d("sync", lambda s: s.dma_start(out=tn.d_y1, in_=tn.y1_scr), q="DS",
            waits=[("p3sp", t2) for t2 in range(NTT)])
        p.d("sync", lambda s: s.dma_start(out=tn.d_wgq, in_=tn.wgq_scr), q="DS",
            samegroup=True)
        p.d("sync", lambda s: s.dma_start(out=tn.d_wuq, in_=tn.wuq_scr), q="DS",
            samegroup=True)
        p.d("sync", lambda s: s.dma_start(out=tn.d_wdq, in_=tn.wdq_scr), q="DS",
            samegroup=True)
        p.d("sync", lambda s: s.dma_start(out=tn.d_xqT, in_=tn.xqT_scr), q="DS",
            samegroup=True, key="dbgsp2")

    def emit_gumm(t, ns):
        tb = t % 2
        w0 = ns * 512
        w1 = min(FI, w0 + 512)
        wd_ = w1 - w0
        guw = [("p4x", t)]
        if ns == 0:
            guw.append("wgT")
            guw.append("wguT")
        prev = ("guev", t, ns - 2) if ns >= 2 else \
            (("guev", t - 1, NS - 2 + ns) if t >= 1 else None)
        if prev is not None:
            guw.append(prev)
        if t == 0 and ns == 1:
            guw.append(("ostt", NTT - 1))

        def gumm(te, t=t, tb=tb, ns=ns, w0=w0, w1=w1, wd_=wd_):
            gps = tn.g_ps if ns % 2 == 0 else tn.op_ps_g
            ups = tn.u_ps if ns % 2 == 0 else tn.op_ps_u
            for hc in range(8):
                te.matmul(gps[:, 0:wd_],
                          tn.xqT4_t[:, tb * 1024 + hc * 128:
                                    tb * 1024 + (hc + 1) * 128],
                          tn.wgT[:, hc * FI + w0: hc * FI + w1],
                          start=(hc == 0), stop=(hc == 7))
            last = None
            for hc in range(8):
                last = te.matmul(ups[:, 0:wd_],
                                 tn.xqT4_t[:, tb * 1024 + hc * 128:
                                           tb * 1024 + (hc + 1) * 128],
                                 tn.wuT[:, hc * FI + w0: hc * FI + w1],
                                 start=(hc == 0), stop=(hc == 7))
            return last
        p.e("tensor", gumm, inc="T", key=("gumm", t, ns), waits=guw)

    def emit_hq_half(t, half):
        o0 = half * (FI // 2)
        hw = FI // 2
        nb0 = o0 // 16

        def hquant(v, o0=o0, hw=hw, nb0=nb0):
            v.tensor_tensor(out=tn.hg_t[:, o0:o0 + hw],
                            in0=tn.hg_t[:, o0:o0 + hw],
                            in1=tn.hu_t[:, o0:o0 + hw], op=ALU.mult)
            quant_prep(v, tn.hg_t[:, o0:o0 + hw], hw,
                       tn.hbm[:, nb0:nb0 + hw // 16],
                       tn.hbm2[:, nb0:nb0 + hw // 16], tn.hq_u2)
            xin = tn.hg_t[:, o0:o0 + hw]
            quant_stairs(v, xin, tn.hq_t[:, o0:o0 + hw], hw,
                         tn.hbm2[:, nb0:nb0 + hw // 16],
                         tn.hbm[:, nb0:nb0 + hw // 16],
                         tn.hu_t, tn.hq_u2, xin)
            return v.tensor_scalar(out=tn.hbm2[:, 0:8], in0=tn.hbm[:, 0:8],
                                   scalar1=1.0, scalar2=None,
                                   op0=ALU.mult)
        hqw = [("silu", t, 3)] if half == 0 else [("guev", t, NS - 1)]
        p.e("vector", hquant, inc="V", key=("hq", t, half), waits=hqw)

    def emit_p4x(t):
        if t >= NTT:
            return
        tb = t % 2
        lx = [("xqspill", t)]
        if t >= 2:
            lx.append(("gumm", t - 2, NS - 1))
        p.d("sync", lambda s, t=t, tb=tb: s.dma_start(
            out=tn.xqT4_t[:, tb * 1024:(tb + 1) * 1024].rearrange(
                "p (h q) -> p h q", h=8),
            in_=tn.xqT_scr.rearrange("(h pp) q -> pp h q", pp=128)[
                :, :, t * 128:(t + 1) * 128]), q="DS", waits=lx,
            key=("p4x", t))

    emit_p4x(0)
    emit_p4x(1)
    for t in range(NTT):
        tb = t % 2
        emit_p4x(t + 2)
        ly = [("p3sp", t)]
        if t >= 1:
            ly.append(("dstt", t - 1))
        p.d("sync", lambda s, t=t: s.dma_start(
            out=tn.y14_t[:], in_=tn.y1_scr[t * 128:(t + 1) * 128, :]), q="DS",
            key=("p4y", t), waits=ly)

        if t == 0:
            emit_gumm(0, 0)
            emit_gumm(0, 1)
        for ns in range(2, NS):
            emit_gumm(t, ns)
        if t + 1 < NTT:
            emit_gumm(t + 1, 0)
            emit_gumm(t + 1, 1)

        for ns in range(NS):
            w0 = ns * 512
            w1 = min(FI, w0 + 512)
            wd_ = w1 - w0
            sw = [("gumm", t, ns)]
            if t == 0 and ns == 0:
                sw.append(("xq", NTT - 1))
            if t >= 1 and ns == 0:
                sw.append(("hq", t - 1, 1))
            p.e("scalar", lambda s, ns=ns, w0=w0, w1=w1, wd_=wd_: s.activation(
                out=tn.hg_t[:, w0:w1],
                in_=(tn.g_ps if ns % 2 == 0 else tn.op_ps_g)[:, 0:wd_],
                func=AF.Silu), inc="SC", key=("silu", t, ns),
                waits=sw)
            def guevS(s, ns=ns, w0=w0, w1=w1, wd_=wd_, last=(ns == NS - 1)):
                r = s.activation(out=tn.hu_t[:, w0:w1],
                                 in_=(tn.u_ps if ns % 2 == 0
                                      else tn.op_ps_u)[:, 0:wd_], func=AF.Copy)
                if not last:
                    return r
                return s.activation(out=tn.sqjunk[:, 0:16],
                                    in_=tn.hu_t[:, 0:16], func=AF.Copy)
            p.e("scalar", guevS, inc="SC", key=("guev", t, ns),
                waits=[("gumm", t, ns)])
            if ns == 2:
                emit_hq_half(t, 0)
        emit_hq_half(t, 1)

        for grp in range(3):
            c0, c1 = grp * 8, min(22, grp * 8 + 8)
            if grp == 0:
                hw_ = [("hq", t, 0)]
                if t >= 1:
                    hw_.append(("dmm", t - 1))
            elif grp == 1:
                hw_ = [("hqtev", t, 0), ("hq", t, 1)]
            else:
                hw_ = [("hqtev", t, 1)]

            def hqt(te, c0=c0, c1=c1):
                last = None
                for ci in range(c0, c1):
                    last = te.transpose(
                        tn.hqT_ps[:, (ci - c0) * 128:(ci - c0 + 1) * 128],
                        tn.hq_t[:, ci * 128:(ci + 1) * 128], tn.ident[:])
                return last
            p.e("tensor", hqt, inc="T", key=("hqT", t, grp), waits=hw_)

            def hqtevS(s, c0=c0, c1=c1, last=(grp == 2)):
                r = s.activation(out=tn.hqT_t[:, c0 * 128:c1 * 128],
                                 in_=tn.hqT_ps[:, 0:(c1 - c0) * 128],
                                 func=AF.Copy)
                if not last:
                    return r
                return s.activation(out=tn.sqjunk[:, 0:16],
                                    in_=tn.hqT_t[:, 0:16], func=AF.Copy)
            p.e("scalar", hqtevS, inc="SC", key=("hqtev", t, grp),
                waits=[("hqT", t, grp)])

        dw = ["wdT", ("hqtev", t, 2)]
        if t >= 1:
            dw.append(("dstt", t - 1))

        def dmm(te):
            last = None
            for half in range(2):
                for ci in range(22):
                    last = te.matmul(
                        tn.d_ps[:, half * 512:(half + 1) * 512],
                        tn.hqT_t[:, ci * 128:(ci + 1) * 128],
                        tn.wdT[:, ci * H + half * 512: ci * H + half * 512 + 512],
                        start=(ci == 0), stop=(ci == 21))
            return last
        p.e("tensor", dmm, inc="T", key=("dmm", t), waits=dw)

        y2f = (lambda: tn.y1_t[:])
        dsw = [("dmm", t), ("p4y", t)]
        if t >= 1:
            dsw.append(("yst", t - 1))
        p.e("vector", lambda v, y2f=y2f: v.scalar_tensor_tensor(
            out=y2f(), in0=tn.d_ps[:], scalar=1.0, in1=tn.y14_t[:],
            op0=ALU.mult, op1=ALU.add, accum_out=tn.lnsum[:, 0:1]),
            inc="V", key=("dstt", t), waits=dsw)
        p.alias(("lnbstt", t), ("dstt", t))
        ln_emit(p, tn, "lnb", t, y2f)
        p.d("sync", lambda s, t=t, y2f=y2f: s.dma_start(
            out=tn.y_out[t * 128:(t + 1) * 128, :], in_=y2f()), q="DS",
            key=("yst", t), waits=[("lnbdone", t)])

    fin = [("yst", t) for t in range(NTT)]
    if DBG:
        fin += ["dbgsp", "dbgsp2"]
    p.e("sync", lambda s: None, waits=fin)


def ln_emit(p, tn, tag, t, yvf):
    p.e("vector", lambda v, yvf=yvf: v.scalar_tensor_tensor(
        out=tn.sqjunk[:], in0=yvf(), scalar=1.0, in1=yvf(),
        op0=ALU.mult, op1=ALU.mult, accum_out=tn.lnsq[:, 0:1]),
        inc="V", key=(tag + "sq", t), waits=[(tag + "stt", t)])

    def varf(v):
        v.tensor_scalar(out=tn.lnsum[:, 1:2], in0=tn.lnsum[:, 0:1],
                        scalar1=1.0 / H, scalar2=None, op0=ALU.mult)
        v.tensor_scalar(out=tn.lnsq[:, 1:2], in0=tn.lnsq[:, 0:1],
                        scalar1=1.0 / H, scalar2=None, op0=ALU.mult)
        v.tensor_tensor(out=tn.lnsum[:, 2:3], in0=tn.lnsum[:, 1:2],
                        in1=tn.lnsum[:, 1:2], op=ALU.mult)
        v.tensor_scalar(out=tn.lnsum[:, 3:4], in0=tn.lnsum[:, 1:2],
                        scalar1=1.0, scalar2=None, op0=ALU.mult)
        return v.tensor_tensor(out=tn.lnsq[:, 2:3], in0=tn.lnsq[:, 1:2],
                               in1=tn.lnsum[:, 2:3], op=ALU.subtract)
    vw = [(tag + "sq", t)]
    if DBG and tag == "lna" and t == 0:
        vw.append("dbg_t0b")
    p.e("vector", varf, inc="V", key=(tag + "var", t), waits=vw)

    def rstdf(s):
        s.activation(out=tn.lnsq[:, 3:4], in_=tn.lnsq[:, 2:3], func=AF.Ln,
                     bias=np.float32(LNEPS))
        return s.activation(out=tn.lnr[:, 0:1], in_=tn.lnsq[:, 3:4],
                            func=AF.Exp, scale=np.float32(-0.5))
    p.e("scalar", rstdf, inc="SC", key=(tag + "rstd", t), waits=[(tag + "var", t)])

    def normf(v, yvf=yvf):
        v.scalar_tensor_tensor(out=tn.lnr[:, 1:2], in0=tn.lnsum[:, 1:2],
                               scalar=-1.0, in1=tn.lnr[:, 0:1],
                               op0=ALU.mult, op1=ALU.mult)
        v.tensor_scalar(out=tn.lnsum[:, 3:4], in0=tn.lnsum[:, 1:2],
                        scalar1=1.0, scalar2=None, op0=ALU.mult)
        v.tensor_scalar(out=yvf(), in0=yvf(), scalar1=tn.lnr[:, 0:1],
                        scalar2=tn.lnr[:, 1:2], op0=ALU.mult, op1=ALU.add)
        return v.tensor_scalar(out=tn.lnsum[:, 3:4], in0=tn.lnsum[:, 1:2],
                               scalar1=1.0, scalar2=None,
                               op0=ALU.mult)
    p.e("vector", normf, inc="V", key=(tag + "done", t),
        waits=[(tag + "rstd", t)])



class TN:
    pass


def build_nc():
    nc = bass.Bass("TRN2", target_bir_lowering=False)
    tn = TN()
    D = tn.__dict__
    D["x_all"] = nc.dram_tensor("x_all", [S, H], F32, kind="ExternalInput").ap()
    D["x_own"] = nc.dram_tensor("x_own", [OWN, H], F32, kind="ExternalInput").ap()
    D["wqkv"] = nc.dram_tensor("wqkv", [H, 3 * H], F32, kind="ExternalInput").ap()
    D["wout"] = nc.dram_tensor("wout", [H, H], F32, kind="ExternalInput").ap()
    D["wg_in"] = nc.dram_tensor("wg_in", [FI, H], F32, kind="ExternalInput").ap()
    D["wu_in"] = nc.dram_tensor("wu_in", [FI, H], F32, kind="ExternalInput").ap()
    D["wd_in"] = nc.dram_tensor("wd_in", [H, FI], F32, kind="ExternalInput").ap()
    D["sintab"] = nc.dram_tensor("sintab", [128, 1024], BF16, kind="ExternalInput").ap()
    D["costab"] = nc.dram_tensor("costab", [128, 1024], BF16, kind="ExternalInput").ap()
    D["ident_in"] = nc.dram_tensor("ident_in", [128, 128], BF16, kind="ExternalInput").ap()
    D["masks_in"] = nc.dram_tensor("masks_in", [NQT * 8 * 128, QT], BF16,
                                   kind="ExternalInput").ap()
    D["cfgq"] = nc.dram_tensor("cfgq", [1, 8], I32, kind="ExternalInput").ap()
    D["y_out"] = nc.dram_tensor("y_out", [OWN, H], F32, kind="ExternalOutput").ap()
    for nm, shp, dt in [
        ("xbf_scr", [S, H], BF16), ("q_scr", [2, S, 128], BF16),
        ("k_scr", [2, S, 128], BF16), ("qown_scr", [2, OWN, 128], BF16),
        ("kperm_scr", [2, S, 128], BF16), ("attnT_scr", [NHP * 128, OWN], BF16),
        ("wqkv_bf", [H, 3 * H], BF16),
        ("wgq_scr", [FI, H], BF16), ("wuq_scr", [FI, H], BF16),
        ("wdq_scr", [H, FI], BF16), ("xqT_scr", [H, OWN], BF16),
        ("y1_scr", [OWN, H], F32),
    ]:
        D[nm] = nc.dram_tensor(nm, shp, dt, kind="Internal").ap()
    if DBG:
        D["d_qk"] = nc.dram_tensor("d_qk", [128, NCH * 256], BF16,
                                   kind="ExternalOutput").ap()
        D["d_vaug"] = nc.dram_tensor("d_vaug", [128, NCH * 130], BF16,
                                     kind="ExternalOutput").ap()
        D["d_attnT"] = nc.dram_tensor("d_attnT", [NHP * 128, OWN], BF16,
                                      kind="ExternalOutput").ap()
        D["d_y1"] = nc.dram_tensor("d_y1", [OWN, H], F32, kind="ExternalOutput").ap()
        D["d_wgq"] = nc.dram_tensor("d_wgq", [FI, H], BF16, kind="ExternalOutput").ap()
        D["d_wuq"] = nc.dram_tensor("d_wuq", [FI, H], BF16, kind="ExternalOutput").ap()
        D["d_wdq"] = nc.dram_tensor("d_wdq", [H, FI], BF16, kind="ExternalOutput").ap()
        D["d_xqT"] = nc.dram_tensor("d_xqT", [H, OWN], BF16, kind="ExternalOutput").ap()
        D["d_aT0"] = nc.dram_tensor("d_aT0", [128, 1024], BF16,
                                    kind="ExternalOutput").ap()
        D["d_ln0"] = nc.dram_tensor("d_ln0", [128, 10], F32,
                                    kind="ExternalOutput").ap()
        D["d_y1t0"] = nc.dram_tensor("d_y1t0", [128, H], F32,
                                     kind="ExternalOutput").ap()
        D["d_xres0"] = nc.dram_tensor("d_xres0", [128, H], F32,
                                      kind="ExternalOutput").ap()
        D["d_ops0"] = nc.dram_tensor("d_ops0", [128, H], F32,
                                     kind="ExternalOutput").ap()

    persist = []

    def sbp(name, shape, dt):
        g = nc.sbuf_tensor(name, shape, dt)
        persist.append(g)
        return g.__enter__()

    D["ident"] = sbp("identb", [128, 128], BF16)
    D["ones64"] = sbp("ones64b", [128, 64], BF16)
    D["cfg_sb"] = sbp("cfg_sb", [1, 8], I32)

    sems = {}
    for n in SEMS:
        g = nc.semaphore("sem_" + n)
        persist.append(g)
        sems[n] = g.__enter__()

    p12 = []

    def sb1(name, shape, dt):
        g = nc.sbuf_tensor(name, shape, dt)
        p12.append(g)
        return g.__enter__()

    def pb1(name, shape, dt):
        g = nc.psum_tensor(name, shape, dt)
        p12.append(g)
        return g.__enter__()

    D["sin_sb"] = sb1("sin_sb", [128, 1024], BF16)
    D["cos_sb"] = sb1("cos_sb", [128, 1024], BF16)
    D["xT"] = sb1("xT", [128, 8 * S], BF16)
    D["wqkv_sb"] = sb1("wqkv_sb", [128, 2 * 8 * 384], BF16)
    D["qk_tok"] = sb1("qk_tok", [128, NCH * 256], BF16)
    D["vaug"] = sb1("vaug", [128, 2 * NCH * 130], BF16)
    D["rope_t1"] = sb1("rope_t1", [128, 1024], BF16)
    D["rope_t2"] = sb1("rope_t2", [128, 1024], BF16)
    D["rope_t3"] = sb1("rope_t3", [128, 1024], BF16)
    D["qT_sb"] = sb1("qT_sb", [128, 2 * OWN], BF16)
    D["kT_sb"] = sb1("kT_sb", [128, 2 * S], BF16)
    D["pT_sb"] = sb1("pT_sb", [128, 4 * 1024], BF16)
    D["mask_sb"] = sb1("mask_sb", [128, 8 * QT], BF16)
    D["rs_sb"] = sb1("rs_sb", [128, QT], F32)
    D["rr_sb"] = sb1("rr_sb", [128, QT], F32)
    D["attnT_sb"] = sb1("attnT_sb", [128, OWN], BF16)
    D["vq_in"] = sb1("vq_in", [128, 4 * VQ_W], F32)
    D["vq_out"] = sb1("vq_out", [128, 2 * VQ_W], BF16)
    D["vq_t"] = sb1("vq_t", [128, VQ_W // 2], F32)
    D["vq_u1"] = sb1("vq_u1", [128, VQ_W // 2], F32)
    D["scjunk"] = sb1("scjunk", [128, 16], BF16)
    D["vq_u2"] = sb1("vq_u2", [128, VQ_W // 2], F32)
    D["vq_bm"] = sb1("vq_bm", [128, VQ_W // 16], F32)
    D["vq_bm2"] = sb1("vq_bm2", [128, VQ_W // 16], F32)
    D["qkv_ps"] = pb1("qkv_ps", [128, 1024], F32)
    D["sco_a"] = pb1("sco_a", [128, 1024], F32)
    D["sco_b"] = pb1("sco_b", [128, 1024], F32)
    D["pv_a"] = pb1("pv_a", [128, 512], F32)
    D["pv_b"] = pb1("pv_b", [128, 512], F32)

    p = P()
    record_p12_only = None

    record(p, tn)

    blk_g = nc.Block()
    blk = blk_g.__enter__()

    @blk.sync
    def _(sync):
        p.replay("sync", sync, sems, 1)

    @blk.vector
    def _(vector):
        p.replay("vector", vector, sems, 1)

    @blk.scalar
    def _(scalar):
        p.replay("scalar", scalar, sems, 1)

    @blk.tensor
    def _(tensor):
        p.replay("tensor", tensor, sems, 1)

    @blk.gpsimd
    def _(gpsimd):
        p.replay("gpsimd", gpsimd, sems, 1)

    blk_g.__exit__(None, None, None)

    for g in reversed(p12):
        g.__exit__(None, None, None)

    p34 = []

    def sb2(name, shape, dt):
        g = nc.sbuf_tensor(name, shape, dt)
        p34.append(g)
        return g.__enter__()

    def pb2(name, shape, dt):
        g = nc.psum_tensor(name, shape, dt)
        p34.append(g)
        return g.__enter__()

    D["wout_sb"] = sb2("wout_sb", [128, 8 * 1024], BF16)
    D["lnsum"] = sb2("lnsum", [128, 4], F32)
    D["lnsq"] = sb2("lnsq", [128, 4], F32)
    D["lnr"] = sb2("lnr", [128, 2], F32)
    D["wgT"] = sb2("wgT", [128, 8 * FI], BF16)
    D["wuT"] = sb2("wuT", [128, 8 * FI], BF16)
    D["wdT"] = sb2("wdT", [128, 22 * H], BF16)
    D["aT_t"] = sb2("aT_t", [128, 1024], BF16)
    D["xres_t"] = sb2("xres_t", [128, H], F32)
    D["y1_t"] = sb2("y1_t", [128, H], F32)
    D["y14_t"] = sb2("y14_t", [128, H], F32)
    D["sqjunk"] = sb2("sqjunk", [128, H], BF16)
    D["xq_t"] = sb2("xq_t", [128, H], BF16)
    D["xbm"] = sb2("xbm", [128, 64], F32)
    D["xbm2"] = sb2("xbm2", [128, 64], F32)
    D["xqT_t"] = sb2("xqT_t", [128, 1024], BF16)
    D["xqT4_t"] = sb2("xqT4_t", [128, 2 * 1024], BF16)
    D["hg_t"] = sb2("hg_t", [128, FI], F32)
    D["hu_t"] = sb2("hu_t", [128, FI], F32)
    D["hq_t"] = sb2("hq_t", [128, FI], BF16)
    D["hbm"] = sb2("hbm", [128, 176], F32)
    D["hbm2"] = sb2("hbm2", [128, 176], F32)
    D["hqT_t"] = sb2("hqT_t", [128, 22 * 128], BF16)
    D["qs1f"] = tn.hqT_t[:].bitcast(F32)
    D["hq_u2"] = tn.wout_sb[:].bitcast(F32)
    D["qs1"] = tn.hg_t
    D["qs2"] = tn.qs1f
    D["qs3"] = tn.hu_t
    D["op_ps"] = pb2("op_ps", [128, 1024], F32)
    D["xqT_ps"] = pb2("xqT_ps", [128, 1024], BF16)
    D["g_ps"] = pb2("g_ps", [128, 512], F32)
    D["u_ps"] = pb2("u_ps", [128, 512], F32)
    D["d_ps"] = pb2("d_ps", [128, 1024], F32)
    D["hqT_ps"] = pb2("hqT_ps", [128, 1024], BF16)
    D["op_ps_g"] = tn.op_ps[:, 0:512]
    D["op_ps_u"] = tn.op_ps[:, 512:1024]

    blk2_g = nc.Block("p34")
    blk2 = blk2_g.__enter__()

    @blk2.sync
    def _(sync):
        p.replay("sync", sync, sems, 2)

    @blk2.vector
    def _(vector):
        p.replay("vector", vector, sems, 2)

    @blk2.scalar
    def _(scalar):
        p.replay("scalar", scalar, sems, 2)

    @blk2.tensor
    def _(tensor):
        p.replay("tensor", tensor, sems, 2)

    @blk2.gpsimd
    def _(gpsimd):
        p.replay("gpsimd", gpsimd, sems, 2)

    blk2_g.__exit__(None, None, None)

    for g in reversed(p34):
        g.__exit__(None, None, None)
    for g in reversed(persist):
        g.__exit__(None, None, None)
    return nc



def host_prep(inputs):
    x = np.asarray(inputs["L0_attn_hidden_states"])
    w_qkv = np.asarray(inputs["L0_attn_w_qkv"])
    w_out = np.asarray(inputs["L0_attn_w_out"])
    w_gate = np.asarray(inputs["L0_ffn_w_gate"])
    w_up = np.asarray(inputs["L0_ffn_w_up"])
    w_down = np.asarray(inputs["L0_ffn_w_down"])

    evens = list(range(0, 32, 2))
    odds = list(range(1, 32, 2))
    perm = []
    for hp in range(NHP):
        ha, hb = 2 * hp, 2 * hp + 1

        def qcols(h, dperm):
            base = 384 * (h // 2) + 64 * (h % 2)
            return [base + d for d in dperm]
        qp = (qcols(ha, evens) + qcols(hb, evens) + qcols(ha, odds)
              + qcols(hb, odds) + qcols(ha, range(32, 48)) + qcols(hb, range(32, 48))
              + qcols(ha, range(48, 64)) + qcols(hb, range(48, 64)))
        kp = [c + 128 for c in qp]
        vp = ([384 * (ha // 2) + 64 * (ha % 2) + 256 + d for d in range(64)]
              + [384 * (hb // 2) + 64 * (hb % 2) + 256 + d for d in range(64)])
        perm += qp + kp + vp
    wqkv_r = np.ascontiguousarray(w_qkv[:, perm])

    inv = 1.0 / 10000.0 ** (np.arange(0, 32, 2, dtype=np.float64) / 32.0)
    ang = np.arange(S, dtype=np.float64)[:, None] * inv[None, :]

    def tab(a):
        a3 = a.reshape(NCH, 128, 16)
        out = np.zeros((128, NCH * 32), dtype=np.float32)
        for c in range(NCH):
            out[:, c * 32:c * 32 + 16] = a3[c]
            out[:, c * 32 + 16:c * 32 + 32] = a3[c]
        return out.astype(BF)
    sintab = tab(np.sin(ang).astype(np.float32))
    costab = tab(np.cos(ang).astype(np.float32))
    ident = np.eye(128, dtype=np.float32).astype(BF)

    in_maps = []
    for c in range(8):
        b, f = c // 2, c % 2
        gq = FOLD_Q[f]
        xb = np.ascontiguousarray(x[b])
        x_own = np.ascontiguousarray(
            np.concatenate([xb[g * QT:(g + 1) * QT] for g in gq], axis=0))
        cfg = np.zeros((1, 8), dtype=np.int32)
        cfg[0, :4] = gq
        masks = np.zeros((NQT, 8, 128, QT), dtype=np.float32)
        for j in range(NQT):
            Dj = DEPTH[j]
            q0 = gq[j] * QT
            for m in range(8):
                i = Dj - 8 + m
                kg = i * 128 + np.arange(128)[:, None]
                qg = q0 + np.arange(QT)[None, :]
                masks[j, m] = (kg <= qg).astype(np.float32)
        in_maps.append({
            "x_all": xb, "x_own": x_own, "wqkv": wqkv_r,
            "wout": np.ascontiguousarray(w_out),
            "wg_in": np.ascontiguousarray(w_gate),
            "wu_in": np.ascontiguousarray(w_up),
            "wd_in": np.ascontiguousarray(w_down),
            "sintab": sintab, "costab": costab, "ident_in": ident,
            "masks_in": masks.reshape(NQT * 8 * 128, QT).astype(BF),
            "cfgq": cfg,
        })
    return in_maps


_NC_CACHE = [None]
_LAST = [None]


def kernel(**inputs):
    in_maps = host_prep(inputs)
    if _NC_CACHE[0] is None:
        _NC_CACHE[0] = build_nc()
    res = run_bass_kernel_spmd(_NC_CACHE[0], in_maps, core_ids=list(range(8)))
    _LAST[0] = res
    out = np.zeros((4, S, H), dtype=np.float32)
    for c in range(8):
        b, f = c // 2, c % 2
        for j, g in enumerate(FOLD_Q[f]):
            out[b, g * QT:(g + 1) * QT] = res.results[c]["y_out"][j * QT:(j + 1) * QT]
    return out
